# version 12
# baseline (speedup 1.0000x reference)
import math
import contextlib
import numpy as np
import ml_dtypes
import concourse.bass as bass
import concourse.mybir as mybir
from concourse.bass_utils import run_bass_kernel_spmd

F32 = mybir.dt.float32
BF16 = mybir.dt.bfloat16
I32 = mybir.dt.int32
AF = mybir.ActivationFunctionType
ALU = mybir.AluOpType
AX = mybir.AxisListType

T = 2048
C = 4096
CK = 32
HPC = 16
HD = 128
DH = 2048
NQG = 4
SCALE = 1.0 / math.sqrt(HD)
SSHIFT = -10.0
DTEMP = 10.0
DSHIFT = -30.0
THR = 0.01
EPS = 1e-5


class Sem:
    __slots__ = ("h", "n")

    def __init__(self, h):
        self.h = h
        self.n = 0


class EQ:
    __slots__ = ("name", "fns")

    def __init__(self, name):
        self.name = name
        self.fns = []

    def __call__(self, fn):
        self.fns.append(fn)


def build_nc():
    nc = bass.Bass("TRN2", target_bir_lowering=False)

    dt_in = lambda n, s, d: nc.dram_tensor(n, s, d, kind="ExternalInput").ap()
    xT = dt_in("xT", [C, T], F32)
    xres = dt_in("xres", [T, DH], F32)
    wq = dt_in("wq", [HPC, 128, CK, 128], F32)
    wk = dt_in("wk", [HPC, 128, CK, 128], F32)
    wv = dt_in("wv", [HPC, 128, CK, 128], F32)
    wo = dt_in("wo", [4, CK, 128, 512], F32)
    wf = dt_in("wf", [4, CK, 128, 512], F32)
    cent = dt_in("cent", [C, 64], F32)
    biasd = dt_in("biasd", [1, DH], F32)
    assignd = dt_in("assignd", [1, DH], I32)
    iota64 = dt_in("iota64", [64, 1], F32)
    idf = dt_in("idf", [128, 128], F32)
    onesf = dt_in("onesf", [128, 128], F32)
    onesb = dt_in("onesb", [128, 1], F32)
    cosT = dt_in("cosT", [128, T], F32)
    sinT2 = dt_in("sinT2", [128, T], F32)
    masks = dt_in("masks", [4, 128, 512], BF16)
    consts = dt_in("consts", [128, 8], F32)

    out = nc.dram_tensor("out", [T, DH], F32, kind="ExternalOutput").ap()

    dt_sc = lambda n, s, d: nc.dram_tensor(n, s, d, kind="Internal").ap()
    qts = dt_sc("qts", [HPC, 128, T], BF16)
    kts = dt_sc("kts", [HPC, 128, T], BF16)
    vfs = dt_sc("vfs", [HPC, 128, T], BF16)
    ffng = dt_sc("ffng", [T, DH], F32)
    ccin = dt_sc("ccin", [HPC * 128, T], BF16)
    ccout = dt_sc("ccout", [2 * HPC * 128, T], BF16)
    cmin = dt_sc("cmin", [64, 1], F32)
    cmout = dt_sc("cmout", [64, 1], F32)

    ccin_r = ccin.rearrange("(h p) t -> h p t", p=128)

    es = contextlib.ExitStack()
    sb = lambda n, s, d: es.enter_context(nc.sbuf_tensor(n, s, d))
    with es:
        XA = sb("XA", [128, CK * T], BF16)
        SCRF = sb("SCRF", [128, 6144], F32)
        SCRB = sb("SCRB", [128, 2048], BF16)
        WR = sb("WR", [128, 6144], BF16)
        cosb = sb("cosb", [128, T], F32)
        sinb = sb("sinb", [128, T], F32)
        maskb = sb("maskb", [128, 2048], BF16)
        DOTS = sb("DOTS", [64, 2048], F32)
        constsb = sb("constsb", [128, 8], F32)
        onesfb = sb("onesfb", [128, 128], F32)
        onesbb = sb("onesbb", [128, 1], BF16)
        iotab = sb("iotab", [64, 1], F32)
        idfb = sb("idfb", [128, 128], F32)
        qgcol = sb("qgcol", [128, 16], F32)
        cmcol = sb("cmcol", [64, 1], F32)
        cmgb = sb("cmgb", [64, 1], F32)
        ROWB = sb("ROWB", [128, 4096], BF16)

        psA = es.enter_context(nc.psum_tensor([128, 2048], F32))
        psB = es.enter_context(nc.psum_tensor([128, 2048], F32))

        sm = lambda n: Sem(es.enter_context(nc.semaphore(n)))
        cc = sm("cc")
        mP = sm("mP")
        mA = sm("mA")
        mV = sm("mV")

        TE, AC, VE, GP, SY = EQ("TE"), EQ("AC"), EQ("VE"), EQ("GP"), EQ("SY")
        WAITED = {}

        def W(q, s, v):
            if v is None or v <= 0:
                return
            k = (q.name, id(s))
            if WAITED.get(k, -1) >= v:
                return
            WAITED[k] = v
            q(lambda e, _h=s.h, _v=v: e.wait_ge(_h, _v))

        def OP(q, build, s=None, amt=1):
            if s is None:
                q(build)
                return None
            s.n += amt
            tgt = s.n
            q(lambda e, _b=build, _h=s.h, _a=amt: _b(e).then_inc(_h, _a))
            return tgt

        class LanePool:

            def __init__(self, q, nlanes, name):
                self.q = q
                self.sems = [sm(f"{name}{i}") for i in range(nlanes)]
                self.i = 0

            def dma(self, build):
                s = self.sems[self.i % len(self.sems)]
                self.i += 1
                if s.n > 0:
                    W(self.q, s, s.n)
                tgt = OP(self.q, build, s, 16)
                return (s, tgt)

            def mark(self):
                return [(s, s.n) for s in self.sems if s.n > 0]

        def WD(q, tok):
            if tok is None:
                return
            if isinstance(tok, list):
                for s, v in tok:
                    W(q, s, v)
            else:
                W(q, tok[0], tok[1])

        PG = LanePool(GP, 6, "dG")
        PS = LanePool(SY, 6, "dS")
        PA = LanePool(AC, 4, "dA")

        def xa(k):
            return XA[:, k * T:(k + 1) * T]

        smalls = [
            (cosb[:, :], cosT), (sinb[:, :], sinT2),
            (maskb[:, :].rearrange("p (m j) -> p m j", j=512),
             masks.rearrange("m p j -> p m j")),
            (constsb[:, :], consts), (onesfb[:, :], onesf),
            (iotab[:, :], iota64), (idfb[:, :], idf),
        ]
        tok_small = []
        for o_, i_ in smalls:
            tok_small.append(PS.dma(
                lambda e, _o=o_, _i=i_: e.dma_start(out=_o, in_=_i)))
        tok_onesb = PG.dma(lambda e: e.dma_start(out=onesbb[:, :], in_=onesb))
        tok_bias = PG.dma(lambda e: e.dma_start(out=ROWB[0:1, 0:2048], in_=biasd))
        tok_cent = PG.dma(lambda e: e.dma_start(
            out=WR[:, 0:2048].rearrange("p (k j) -> p k j", j=64),
            in_=cent.rearrange("(k p) j -> p k j", p=128)))
        tok_xa = []
        for k in range(CK):
            tok_xa.append(PG.dma(lambda e, _k=k: e.dma_start(
                out=xa(_k), in_=xT[_k * 128:(_k + 1) * 128, :])))

        WD(AC, tok_onesb)
        t_sq_mm = [None] * CK
        for k in range(CK):
            WD(AC, tok_xa[k])
            if k >= 1:
                W(AC, mP, t_sq_mm[k - 1])
            t_sq = OP(AC, lambda e, _k=k: e.activation(
                SCRB[:, :], xa(_k), AF.Square), mA, 1)
            WD(TE, tok_onesb)
            WD(TE, tok_xa[k])
            for j in range(4):
                OP(TE, lambda e, _k=k, _j=j: e.matmul(
                    psA[0:1, _j * 512:(_j + 1) * 512], onesbb[:, :],
                    xa(_k)[:, _j * 512:(_j + 1) * 512],
                    start=(_k == 0), stop=(_k == CK - 1)))
            W(TE, mA, t_sq)
            for j in range(4):
                t_sq_mm[k] = OP(TE, lambda e, _k=k, _j=j: e.matmul(
                    psB[0:1, _j * 512:(_j + 1) * 512], onesbb[:, :],
                    SCRB[:, _j * 512:(_j + 1) * 512],
                    start=(_k == 0), stop=(_k == CK - 1)),
                    mP if j == 3 else None, 1)
        t_stats = t_sq_mm[CK - 1]

        mean_r = SCRF[0:1, 0:2048]
        rstd_r = SCRF[0:1, 2048:4096]
        tmp_r = SCRF[0:1, 4096:6144]
        u_r = DOTS[0:1, 0:2048]
        W(VE, mP, t_stats)
        t_mean = OP(VE, lambda e: e.tensor_single_scalar(
            mean_r, psA[0:1, :], 1.0 / C, ALU.mult), mV, 1)
        t_msq = OP(VE, lambda e: e.tensor_single_scalar(
            tmp_r, psB[0:1, :], 1.0 / C, ALU.mult), mV, 1)
        W(VE, mV, t_msq)
        t_m2 = OP(VE, lambda e: e.tensor_tensor(
            rstd_r, mean_r, mean_r, ALU.mult), mV, 1)
        W(VE, mV, t_m2)
        t_var = OP(VE, lambda e: e.tensor_tensor(
            tmp_r, tmp_r, rstd_r, ALU.subtract), mV, 1)
        WD(AC, tok_small[3])
        W(AC, mV, t_var)
        t_ln = OP(AC, lambda e: e.activation(
            u_r, tmp_r, AF.Ln, bias=constsb[0:1, 4:5]), mA, 1)
        W(AC, mA, t_ln)
        t_rstd = OP(AC, lambda e: e.activation(
            rstd_r, u_r, AF.Exp, scale=constsb[0:1, 5:6]), mA, 1)

        WD(TE, tok_small[4])
        W(TE, mV, t_msq)
        for j in range(4):
            OP(TE, lambda e, _j=j: e.matmul(
                psB[:, _j * 512:(_j + 1) * 512], onesfb[0:1, 0:128],
                mean_r[:, _j * 512:(_j + 1) * 512], start=True, stop=True))
        W(TE, mA, t_rstd)
        t_bcast = None
        for j in range(4):
            t_bcast = OP(TE, lambda e, _j=j: e.matmul(
                psA[:, _j * 512:(_j + 1) * 512], onesfb[0:1, 0:128],
                rstd_r[:, _j * 512:(_j + 1) * 512], start=True, stop=True),
                mP if j == 3 else None, 1)

        tok_xf = [None] * CK
        t_xn = [None] * CK
        t_sub = [None] * CK
        for k in range(CK):
            slot = SCRF[:, (k % 2) * 2048:(k % 2) * 2048 + 2048]
            if k >= 2:
                W(SY, mV, t_xn[k - 2])
            else:
                W(SY, mP, t_bcast)
            tok_xf[k] = PS.dma(lambda e, _k=k, _s=slot: e.dma_start(
                out=_s, in_=xT[_k * 128:(_k + 1) * 128, :]))
            WD(VE, tok_xf[k])
            W(VE, mP, t_bcast)
            t_sub[k] = OP(VE, lambda e, _s=slot: e.tensor_tensor(
                _s, _s, psB[:, 0:2048], ALU.subtract), mV, 1)
            W(VE, mV, t_sub[k])
            t_xn[k] = OP(VE, lambda e, _k=k, _s=slot: e.tensor_tensor(
                xa(_k), _s, psA[:, 0:2048], ALU.mult), mV, 1)

        W(TE, mV, t_xn[CK - 1])
        WD(TE, tok_cent)
        t_dmm = None
        for k in range(CK):
            for j in range(4):
                t_dmm = OP(TE, lambda e, _k=k, _j=j: e.matmul(
                    psA[0:64, _j * 512:(_j + 1) * 512],
                    WR[:, _k * 64:(_k + 1) * 64],
                    xa(_k)[:, _j * 512:(_j + 1) * 512],
                    start=(_k == 0), stop=(_k == CK - 1)),
                    mP if (k == CK - 1 and j == 3) else None, 1)
        W(AC, mP, t_dmm)
        t_dexp = OP(AC, lambda e: e.activation(
            DOTS[:, :], psA[0:64, 0:2048], AF.Exp,
            bias=constsb[0:64, 3:4], scale=constsb[0:64, 2:3]), mA, 1)
        W(TE, mA, t_dexp)
        W(TE, mV, t_sub[CK - 1])
        t_dsum = None
        for j in range(4):
            t_dsum = OP(TE, lambda e, _j=j: e.matmul(
                psB[0:1, _j * 512:(_j + 1) * 512], onesfb[0:64, 0:1],
                DOTS[:, _j * 512:(_j + 1) * 512], start=True, stop=True),
                mP if j == 3 else None, 1)
        W(VE, mP, t_dsum)
        t_thr = OP(VE, lambda e: e.tensor_single_scalar(
            tmp_r, psB[0:1, :], THR, ALU.mult), mV, 1)
        W(TE, mV, t_thr)
        t_thb = None
        for j in range(4):
            t_thb = OP(TE, lambda e, _j=j: e.matmul(
                psB[0:64, _j * 512:(_j + 1) * 512], onesfb[0:1, 0:64],
                tmp_r[:, _j * 512:(_j + 1) * 512], start=True, stop=True),
                mP if j == 3 else None, 1)
        W(VE, mP, t_thb)
        t_diff = OP(VE, lambda e: e.tensor_tensor(
            DOTS[:, :], DOTS[:, :], psB[0:64, 0:2048], ALU.subtract), mV, 1)
        W(VE, mV, t_diff)
        t_cm = OP(VE, lambda e: e.reduce_max(
            cmcol[:, :], DOTS[:, :], axis=AX.X), mV, 1)
        W(SY, mV, t_cm)
        tok_cmw = PS.dma(lambda e: e.dma_start(out=cmin, in_=cmcol[:, :]))
        WD(GP, tok_cmw)
        t_cc1 = OP(GP, lambda e: e.collective_compute(
            "AllReduce", ALU.max, replica_groups=[list(range(8))],
            ins=[cmin.opt()], outs=[cmout.opt()]), cc, 1)

        WD(TE, tok_small[6])
        t_qtr = [None] * 16
        for tt in range(16):
            W(TE, mV, t_cm)
            t_qtr[tt] = OP(TE, lambda e, _tt=tt: e.transpose(
                psA[:, _tt * 64:(_tt + 1) * 64],
                DOTS[:, _tt * 128:(_tt + 1) * 128], idfb[0:64, 0:64]), mP, 1)
        t_qg = [None] * 16
        for tt in range(16):
            W(VE, mP, t_qtr[15])
            t1 = OP(VE, lambda e, _tt=tt: e.reduce_max(
                qgcol[:, _tt:_tt + 1], psA[:, _tt * 64:(_tt + 1) * 64],
                axis=AX.X), mV, 1)
            W(VE, mV, t1)
            t_qg[tt] = OP(VE, lambda e, _tt=tt: e.tensor_single_scalar(
                qgcol[:, _tt:_tt + 1], qgcol[:, _tt:_tt + 1], 0.0, ALU.is_gt),
                mV, 1)
        t_qr = None
        for tt in range(16):
            W(TE, mV, t_qg[tt])
            t_qr = OP(TE, lambda e, _tt=tt: e.matmul(
                psB[0:1, _tt * 128:(_tt + 1) * 128], qgcol[:, _tt:_tt + 1],
                idfb[:, :], start=True, stop=True),
                mP if tt == 15 else None, 1)
        W(AC, mP, t_qr)
        t_qrow = OP(AC, lambda e: e.activation(
            ROWB[0:1, 2048:4096], psB[0:1, 0:2048], AF.Copy), mA, 1)

        QF = SCRF[:, 0:2048]
        ROT = SCRF[:, 2048:4096]
        ROT2 = SCRF[:, 4096:6144]
        QRO = SCRB[:, 0:2048]

        mats = []
        for h in range(HPC):
            mats.append(("q", h, wq, qts))
            mats.append(("k", h, wk, kts))
            mats.append(("v", h, wv, vfs))

        wslot_owner = [None, None, None]
        t_mat = [None] * len(mats)
        t_rope = [None] * len(mats)
        t_ract = [None] * len(mats)
        tok_spill = [None] * len(mats)
        t_rot_read = 0
        t_qf_read = 0
        for mi, (kind, h, wsrc, dst) in enumerate(mats):
            tok_wh = [None, None]
            for half in range(2):
                sl = (2 * mi + half) % 3
                if wslot_owner[sl] is not None:
                    W(GP, mP, wslot_owner[sl])
                tok_wh[half] = PG.dma(lambda e, _w=wsrc, _h=h, _hf=half, _sl=sl:
                    e.dma_start(
                        out=WR[:, _sl * 2048:(_sl + 1) * 2048].rearrange(
                            "p (k j) -> p k j", j=128),
                        in_=_w[_h][:, _hf * 16:(_hf + 1) * 16, :]))

            psP = psA if (mi % 2 == 0) else psB
            if mi == 0:
                W(TE, mV, t_qg[15])
            elif mi == 1:
                W(TE, mA, t_qrow)
                W(TE, mV, t_diff)
            else:
                W(TE, mV, t_rope[mi - 2])
                W(TE, mA, t_ract[mi - 2])
            t_half_done = [None, None]
            for k in range(CK):
                half = k // 16
                sl = (2 * mi + half) % 3
                WD(TE, tok_wh[half])
                for j in range(4):
                    inc = (j == 3 and (k == 15 or k == CK - 1))
                    t = OP(TE, lambda e, _p=psP, _k=k, _j=j, _sl=sl:
                        e.matmul(
                            _p[:, _j * 512:(_j + 1) * 512],
                            WR[:, _sl * 2048 + (_k % 16) * 128:
                               _sl * 2048 + (_k % 16) * 128 + 128],
                            xa(_k)[:, _j * 512:(_j + 1) * 512],
                            start=(_k == 0), stop=(_k == CK - 1)),
                        mP if inc else None, 1)
                    if inc:
                        t_half_done[half] = t
            t_mat[mi] = t_half_done[1]
            wslot_owner[(2 * mi) % 3] = t_half_done[0]
            wslot_owner[(2 * mi + 1) % 3] = t_half_done[1]

            if kind == "v":
                W(AC, mP, t_mat[mi])
                if mi >= 1:
                    WD(AC, tok_spill[mi - 1])
                t_ract[mi] = OP(AC, lambda e, _p=psP: e.activation(
                    QRO, _p[:, 0:2048], AF.Copy), mA, 1)
                t_rope[mi] = mV.n
                W(SY, mA, t_ract[mi])
                tok_spill[mi] = PS.dma(lambda e, _h=h, _d=dst: e.dma_start(
                    out=_d[_h], in_=QRO))
            else:
                W(AC, mP, t_mat[mi])
                W(AC, mV, t_rot_read)
                OP(AC, lambda e, _p=psP: e.activation(
                    ROT[0:64, :], _p[64:128, 0:2048], AF.Copy))
                t_ract[mi] = OP(AC, lambda e, _p=psP: e.activation(
                    ROT[64:128, :], _p[0:64, 0:2048], AF.Copy), mA, 1)
                WD(VE, tok_small[0])
                WD(VE, tok_small[1])
                W(VE, mP, t_mat[mi])
                W(VE, mV, t_qf_read)
                W(VE, mA, t_ract[mi])
                t1 = OP(VE, lambda e, _p=psP: e.tensor_tensor(
                    QF, _p[:, 0:2048], cosb[:, :], ALU.mult), mV, 1)
                t_rope[mi] = t1
                W(VE, mA, t_ract[mi])
                t2 = OP(VE, lambda e: e.tensor_tensor(
                    ROT2, ROT, sinb[:, :], ALU.mult), mV, 1)
                t_rot_read = t2
                W(VE, mV, t2)
                if mi >= 1:
                    WD(VE, tok_spill[mi - 1])
                t3 = OP(VE, lambda e: e.tensor_tensor(
                    QRO, QF, ROT2, ALU.add), mV, 1)
                t_qf_read = t3
                W(SY, mV, t3)
                tok_spill[mi] = PS.dma(lambda e, _h=h, _d=dst: e.dma_start(
                    out=_d[_h], in_=QRO))

        W(SY, cc, t_cc1)
        tok_cmr = PS.dma(lambda e: e.dma_start(out=cmgb[:, :], in_=cmout))
        assignf = SCRF[0:1, 0:2048]
        WD(GP, tok_spill[-2])
        tok_asf = PG.dma(lambda e: e.dma_start(out=assignf, in_=assignd))
        WD(TE, tok_asf)
        W(TE, mA, t_ract[-1])
        W(TE, mV, t_rope[-1] or 0)
        t_ab = None
        for j in range(4):
            t_ab = OP(TE, lambda e, _j=j: e.matmul(
                psB[0:64, _j * 512:(_j + 1) * 512], onesfb[0:1, 0:64],
                assignf[:, _j * 512:(_j + 1) * 512], start=True, stop=True),
                mP if j == 3 else None, 1)
        WD(VE, tok_small[5])
        W(VE, mP, t_ab)
        onehot = SCRF[0:64, 0:2048]
        t_oh = OP(VE, lambda e: e.tensor_scalar(
            onehot, psB[0:64, 0:2048], iotab[:, :], None, ALU.is_equal), mV, 1)
        W(TE, mV, t_oh)
        WD(TE, tok_cmr)
        t_rgm = None
        for j in range(4):
            t_rgm = OP(TE, lambda e, _j=j: e.matmul(
                psB[0:1, _j * 512:(_j + 1) * 512], cmgb[:, :],
                onehot[:, _j * 512:(_j + 1) * 512], start=True, stop=True),
                mP if j == 3 else None, 1)
        W(VE, mP, t_rgm)
        t_rg01 = OP(VE, lambda e: e.tensor_single_scalar(
            tmp_r, psB[0:1, :], 0.0, ALU.is_gt), mV, 1)
        W(TE, mV, t_rg01)
        t_rgb = None
        for j in range(4):
            t_rgb = OP(TE, lambda e, _j=j: e.matmul(
                psB[:, _j * 512:(_j + 1) * 512], onesfb[0:1, 0:128],
                tmp_r[:, _j * 512:(_j + 1) * 512], start=True, stop=True),
                mP if j == 3 else None, 1)
        W(AC, mP, t_rgb)
        rgb = SCRF[:, 0:2048]
        t_rgbc = OP(AC, lambda e: e.activation(
            rgb, psB[:, 0:2048], AF.Copy), mA, 1)

        def ptile(tt):
            return (psA if tt < 4 else psB)[:, (tt % 4) * 512:(tt % 4) * 512 + 512]

        wtile_owner = [t_mat[-1]] * 12
        tok_stage = [None, None]
        t_ffn_evac = {}
        t_ffn_xa_low = None
        t_ffn_xa_low2 = None
        for th in range(2):
            for dch in range(4):
                gi = th * 4 + dch
                t_g = [None] * 8
                for k in range(CK):
                    sl = (gi * CK + k) % 12
                    if wtile_owner[sl] is not None:
                        W(GP, mP, wtile_owner[sl])
                    tok_wt = PG.dma(lambda e, _d=dch, _k=k, _sl=sl:
                        e.dma_start(out=WR[:, _sl * 512:(_sl + 1) * 512],
                                    in_=wf[_d, _k]))
                    WD(TE, tok_wt)
                    if gi == 0 and k == 0:
                        W(TE, mA, t_rgbc)
                    for tt in range(8):
                        if gi >= 1 and k == 0:
                            W(TE, mV, t_ffn_evac.get((gi - 1, tt)))
                        t_g[tt] = OP(TE, lambda e, _th=th, _k=k, _tt=tt, _sl=sl:
                            e.matmul(
                                ptile(_tt),
                                XA[:, _k * T + (_th * 8 + _tt) * 128:
                                   _k * T + (_th * 8 + _tt) * 128 + 128],
                                WR[:, _sl * 512:(_sl + 1) * 512],
                                start=(_k == 0), stop=False),
                            mP if tt == 7 else None, 1)
                    wtile_owner[sl] = t_g[7]
                    if gi == 7 and k == 23:
                        t_ffn_xa_low = t_g[7]
                    if gi == 7 and k == 26:
                        t_ffn_xa_low2 = t_g[7]
                W(TE, mA, t_qrow)
                WD(TE, tok_bias)
                for tt in range(8):
                    t_g[tt] = OP(TE, lambda e, _th=th, _d=dch, _tt=tt: e.matmul(
                        ptile(_tt),
                        ROWB[0:1, 2048 + (_th * 8 + _tt) * 128:
                             2048 + (_th * 8 + _tt) * 128 + 128],
                        ROWB[0:1, _d * 512:(_d + 1) * 512],
                        start=False, stop=True), mP, 1)
                for tt in range(8):
                    ss = tt % 2
                    stage = SCRF[:, 2048 + ss * 512:2048 + ss * 512 + 512]
                    W(VE, mP, t_g[tt])
                    W(VE, mA, t_rgbc)
                    WD(VE, tok_stage[ss])
                    t_ffn_evac[(gi, tt)] = OP(VE, lambda e, _th=th, _d=dch,
                                              _tt=tt, _st=stage:
                        e.scalar_tensor_tensor(
                            _st, ptile(_tt),
                            qgcol[:, _th * 8 + _tt:_th * 8 + _tt + 1],
                            rgb[:, _d * 512:(_d + 1) * 512],
                            ALU.mult, ALU.mult), mV, 1)
                    W(AC, mV, t_ffn_evac[(gi, tt)])
                    tok_stage[ss] = PA.dma(lambda e, _th=th, _d=dch, _tt=tt,
                                           _st=stage: e.dma_start(
                        out=ffng[(_th * 8 + _tt) * 128:(_th * 8 + _tt) * 128 + 128,
                                 _d * 512:(_d + 1) * 512], in_=_st))
        tok_ffn_done = PA.mark()
        t_ffn_pe = mP.n
        t_ffn_ve = t_ffn_evac[(7, 7)]

        AB = 43008
        def hslot(h, which):
            base = AB + (h % 2) * 6144
            off = {"q": 0, "k": 2048, "v": 4096}[which]
            return XA[:, base + off:base + off + 2048]

        def pslot(kc):
            return XA[:, AB + 12288 + kc * 512:AB + 12288 + kc * 512 + 512]

        def aoslot(par):
            return XA[:, AB + 20480 + par * 512:AB + 20480 + par * 512 + 512]

        tok_head_load = [None] * HPC
        t_head_pe = [None] * HPC
        t_exp = {}
        t_msk = {}
        t_avdone = {}
        t_csread = {}
        t_rbs_d = {}
        t_rc_d = {}
        tok_aow = {}
        last_exp_on_bank = [None] * 4
        last_av_on_slot = [None] * 16
        last_msk_on_slot = [None] * 16
        last_exp_on_slot = [None] * 16
        t_rbc_last = None
        recrow = lambda par: SCRF[0:1, 3072 + par * 512:3072 + par * 512 + 512]
        rbs = lambda par: SCRF[:, 4096 + par * 512:4096 + par * 512 + 512]

        def emit_head_load(h):
            WD(SY, tok_spill[3 * h])
            WD(SY, tok_spill[3 * h + 1])
            WD(SY, tok_spill[3 * h + 2])
            W(SY, mP, t_head_pe[h - 2] if h >= 2 else
              (t_ffn_xa_low if h == 0 else t_ffn_xa_low2))
            PS.dma(lambda e, _h=h: e.dma_start(out=hslot(_h, "q"), in_=qts[_h]))
            PS.dma(lambda e, _h=h: e.dma_start(out=hslot(_h, "k"), in_=kts[_h]))
            for kt in range(16):
                PS.dma(lambda e, _h=h, _kt=kt: e.dma_start(
                    out=hslot(_h, "v")[:, _kt * 128:(_kt + 1) * 128],
                    in_=vfs[_h][:, _kt * 128:(_kt + 1) * 128],
                    transpose=True))
            tok_head_load[h] = PS.mark()

        W(TE, mV, t_ffn_ve)
        pend_rbc = [None]
        tok_aT_chunk = {}
        AG_START = [0, 6, 12, 15]
        AG_NH = [6, 6, 3, 1]
        AG_OUT = [0, 1536, 3072, 3840]
        AT_EARLY = [(0, 0, 0, 6), (1, 0, 6, 6), (2, 0, 12, 3), (3, 0, 15, 1),
                    (0, 1, 16, 5)]
        AT_LATE = [(0, 1, 21, 1), (1, 1, 22, 6), (2, 1, 28, 3), (3, 1, 31, 1)]

        def emit_ag(i):
            if pend_rbc[0] is not None:
                pend_rbc[0]()
                pend_rbc[0] = None
            for s_, v_ in PS.mark():
                W(GP, s_, v_)
            W(GP, cc, 1 + i)
            if i == 0:
                W(GP, mP, t_ffn_pe)
            OP(GP, lambda e, _i=i: e.collective_compute(
                "AllGather", ALU.bypass,
                replica_groups=[[0, 1], [2, 3], [4, 5], [6, 7]],
                ins=[ccin[AG_START[_i] * 128:(AG_START[_i] + AG_NH[_i]) * 128,
                          :].opt()],
                outs=[ccout[AG_OUT[_i]:AG_OUT[_i] + 2 * AG_NH[_i] * 128,
                            :].opt()]), cc, 1)
            for (ii, r, g0, nh) in AT_EARLY:
                if ii != i:
                    continue
                W(GP, cc, 2 + i)
                ro = (AG_OUT[ii] + r * AG_NH[ii] * 128
                      + (g0 - r * 16 - AG_START[ii]) * 128)
                tok = PG.dma(lambda e, _ro=ro, _g0=g0, _nh=nh:
                    e.dma_start(
                        out=XA[:, _g0 * 2048:(_g0 + _nh) * 2048].rearrange(
                            "p (h t) -> p h t", t=T),
                        in_=ccout[_ro:_ro + _nh * 128, :].rearrange(
                            "(h p) t -> p h t", p=128)))
                for g in range(g0, g0 + nh):
                    tok_aT_chunk[g] = tok

        W(TE, mV, t_ffn_ve)
        emit_head_load(0)
        for h in range(HPC):
            if h + 1 < HPC:
                emit_head_load(h + 1)
            qb, kb, vb = hslot(h, "q"), hslot(h, "k"), hslot(h, "v")
            t_sc = {}

            def AVP(qg):
                par = qg % 2
                return psB[:, (0 if par == 0 else 3) * 512:
                           (0 if par == 0 else 3) * 512 + 512]

            CSP = psB[0:1, 512:1024]

            def emit_sc(qg, kc, _h=h, _qb=qb, _kb=kb):
                WD(TE, tok_head_load[_h])
                W(TE, mA, last_exp_on_bank[kc % 4])
                t_sc[(qg, kc)] = OP(TE, lambda e, _kc=kc, _qg=qg: e.matmul(
                    psA[:, (_kc % 4) * 512:(_kc % 4) * 512 + 512],
                    _kb[:, _kc * 128:(_kc + 1) * 128],
                    _qb[:, _qg * 512:(_qg + 1) * 512],
                    start=True, stop=True), mP, 1)

            def emit_exp(qg, kc, _h=h):
                W(AC, mP, t_sc[(qg, kc)])
                W(AC, mP, last_av_on_slot[kc])
                W(AC, mV, last_msk_on_slot[kc])
                W(AC, mA, last_exp_on_slot[kc])
                t_exp[(_h, qg, kc)] = OP(AC, lambda e, _kc=kc: e.activation(
                    pslot(_kc), psA[:, (_kc % 4) * 512:(_kc % 4) * 512 + 512],
                    AF.Exp, bias=constsb[:, 1:2], scale=constsb[:, 0:1]),
                    mA, 1)
                last_exp_on_bank[kc % 4] = t_exp[(_h, qg, kc)]
                last_exp_on_slot[kc] = t_exp[(_h, qg, kc)]
                if kc >= 4 * qg:
                    m = kc - 4 * qg
                    W(VE, mA, t_exp[(_h, qg, kc)])
                    t_msk[(_h, qg, kc)] = OP(VE, lambda e, _kc=kc, _m=m:
                        e.tensor_tensor(
                            pslot(_kc), pslot(_kc),
                            maskb[:, _m * 512:(_m + 1) * 512], ALU.mult),
                        mV, 1)
                    last_msk_on_slot[kc] = t_msk[(_h, qg, kc)]

            def emit_cs(qg, kc, start, _h=h):
                kcn = 4 * (qg + 1)
                if start:
                    pr = (_h, qg - 1) if qg >= 1 else (
                        (_h - 1, 3) if _h >= 1 else None)
                    if pr is not None and pr in t_rc_d:
                        W(TE, mV, t_rc_d[pr])
                return OP(TE, lambda e, _kc=kc, _st=start: e.matmul(
                    CSP, onesbb[:, :], pslot(_kc),
                    start=_st, stop=(_kc == kcn - 1)), mP, 1)

            def emit_csav(qg, kc, _h=h, _vb=vb):
                kcn = 4 * (qg + 1)
                avp = AVP(qg)
                if (_h, qg, kc) in t_msk:
                    W(TE, mV, t_msk[(_h, qg, kc)])
                else:
                    W(TE, mA, t_exp[(_h, qg, kc)])
                if kc == 0:
                    pk = (_h, qg - 2) if qg >= 2 else (
                        (_h - 1, qg + 2) if _h >= 1 else None)
                    if pk is not None:
                        W(TE, mV, t_avdone.get(pk))
                if kc == 1:
                    emit_cs(qg, 1, True)
                    last_av_on_slot[0] = emit_cs(qg, 0, False)
                elif kc >= 2 or kcn == 1:
                    emit_cs(qg, kc, False)
                t = OP(TE, lambda e, _kc=kc, _kcn=kcn, _avp=avp: e.matmul(
                    _avp, _vb[:, _kc * 128:(_kc + 1) * 128], pslot(_kc),
                    start=(_kc == 0), stop=(_kc == _kcn - 1)), mP, 1)
                last_av_on_slot[kc] = t
                return t

            for j in range(4):
                emit_sc(0, j)
                emit_exp(0, j)
            for qg in range(NQG):
                kcn = 4 * (qg + 1)
                par = qg % 2
                avp = AVP(qg)
                t_last_av = None
                for kc in range(kcn):
                    if kc >= 4:
                        emit_exp(qg, kc)
                    t_last_av = emit_csav(qg, kc)
                    if kc == 1 and pend_rbc[0] is not None:
                        t_rbc_last = pend_rbc[0]()
                        pend_rbc[0] = None
                    nxt = kc + 4
                    if nxt < kcn:
                        emit_sc(qg, nxt)
                        emit_exp(qg, nxt)
                    elif qg + 1 < NQG:
                        for j in range(4):
                            if max(kcn - 4 + j, 1 if j <= 1 else j) == kc:
                                emit_sc(qg + 1, j)
                                emit_exp(qg + 1, j)
                t_csread[(h, qg)] = t_last_av
                t_head_pe[h] = t_last_av

                W(VE, mP, t_last_av)
                t_rc = OP(VE, lambda e, _p=par: e.reciprocal(
                    recrow(_p), CSP), mV, 1)
                t_rc_d[(h, qg)] = t_rc

                def emit_rbc(_h=h, _qg=qg, _par=par, _avp=avp, _trc=t_rc):
                    W(TE, mV, _trc)
                    prevq = (_h, _qg - 1) if _qg >= 1 else (
                        (_h - 1, 3) if _h >= 1 else None)
                    if prevq is not None and prevq in t_rbs_d:
                        W(TE, mV, t_rbs_d[prevq][0])
                    t_rbc = OP(TE, lambda e, _p=_par: e.matmul(
                        psB[:, 1024:1536], onesfb[0:1, 0:128], recrow(_p),
                        start=True, stop=True), mP, 1)
                    pk2 = (_h, _qg - 2) if _qg >= 2 else (
                        (_h - 1, _qg + 2) if _h >= 1 else None)
                    W(VE, mP, t_rbc)
                    if pk2 is not None:
                        W(VE, mV, t_avdone.get(pk2))
                    t_rbs = OP(VE, lambda e, _p=_par: e.tensor_copy(
                        rbs(_p), psB[:, 1024:1536]), mV, 1)
                    t_rbs_d[(_h, _qg)] = (t_rbs, t_rbc, t_rc)
                    W(VE, mV, t_rbs)
                    if pk2 is not None:
                        WD(VE, tok_aow.get(pk2))
                    t_av = OP(VE, lambda e, _p=_par, _a=_avp: e.tensor_tensor(
                        aoslot(_p), _a, rbs(_p), ALU.mult), mV, 1)
                    t_avdone[(_h, _qg)] = t_av
                    W(SY, mV, t_av)
                    tok_aow[(_h, _qg)] = PS.dma(lambda e, _hh=_h, _qq=_qg,
                                                _p=_par: e.dma_start(
                        out=ccin_r[_hh][:, _qq * 512:(_qq + 1) * 512],
                        in_=aoslot(_p)))
                    return t_rbc

                pend_rbc[0] = emit_rbc
                if qg == NQG - 1 and h in (5, 11, 14, 15):
                    emit_ag((5, 11, 14, 15).index(h))

        if pend_rbc[0] is not None:
            t_rbc_last = pend_rbc[0]()
            pend_rbc[0] = None

        W(GP, mP, t_head_pe[HPC - 1])
        W(GP, mV, t_avdone[(HPC - 1, NQG - 1)])
        WD(GP, tok_aow.get((HPC - 1, NQG - 1)))
        WD(GP, tok_aow.get((HPC - 1, NQG - 2)))
        for (i, r, g0, nh) in AT_LATE:
            W(GP, cc, 2 + i)
            ro = (AG_OUT[i] + r * AG_NH[i] * 128
                  + (g0 - r * 16 - AG_START[i]) * 128)
            tok = PG.dma(lambda e, _ro=ro, _g0=g0, _nh=nh:
                e.dma_start(
                    out=XA[:, _g0 * 2048:(_g0 + _nh) * 2048].rearrange(
                        "p (h t) -> p h t", t=T),
                    in_=ccout[_ro:_ro + _nh * 128, :].rearrange(
                        "(h p) t -> p h t", p=128)))
            for g in range(g0, g0 + nh):
                tok_aT_chunk[g] = tok

        KORD = [0, 1, 2, 3, 4, 5, 16, 17, 18, 19, 20, 21,
                6, 7, 8, 9, 10, 11, 22, 23, 24, 25, 26, 27,
                12, 13, 14, 28, 29, 30, 15, 31]
        XRt = lambda s: SCRF[:, s * 512:s * 512 + 512]
        FRt = lambda s: SCRF[:, 2048 + s * 512:2048 + s * 512 + 512]
        OUTt = lambda s: SCRF[:, 4096 + s * 512:4096 + s * 512 + 512]
        xr_owner = [None] * 4
        fr_owner = [None] * 4
        tok_out = [None] * 4
        t_o_evac = {}
        wtile_owner = [None] * 12
        tok_xr_g = {}
        tok_fr_g = {}

        def emit_s4_prefetch(j):
            gj, ttj = j // 8, j % 8
            thj, dchj = gj // 4, gj % 4
            sl = ttj % 4
            row = (thj * 8 + ttj) * 128
            col = dchj * 512
            if xr_owner[sl] is not None:
                W(AC, mV, xr_owner[sl])
            tok_xr_g[j] = PA.dma(lambda e, _r=row, _c=col, _sl=sl: e.dma_start(
                out=XRt(_sl), in_=xres[_r:_r + 128, _c:_c + 512]))
            if fr_owner[sl] is not None:
                W(AC, mV, fr_owner[sl])
            WD(AC, tok_ffn_done)
            tok_fr_g[j] = PA.dma(lambda e, _r=row, _c=col, _sl=sl: e.dma_start(
                out=FRt(_sl), in_=ffng[_r:_r + 128, _c:_c + 512]))

        W(AC, mV, t_avdone[(HPC - 1, NQG - 1)])
        W(AC, mP, t_rbc_last)
        for j in range(4):
            emit_s4_prefetch(j)
        for th in range(2):
            for dch in range(4):
                gi = th * 4 + dch
                t_g = [None] * 8
                for k in range(CK):
                    kc_g = KORD[k]
                    sl = (gi * CK + k) % 12
                    if wtile_owner[sl] is not None:
                        W(GP, mP, wtile_owner[sl])
                    tok_wt = PG.dma(lambda e, _d=dch, _k=kc_g, _sl=sl:
                        e.dma_start(out=WR[:, _sl * 512:(_sl + 1) * 512],
                                    in_=wo[_d, _k]))
                    WD(TE, tok_wt)
                    WD(TE, tok_aT_chunk[kc_g])
                    if gi == 0 and k == 0:
                        W(TE, mV, t_avdone[(HPC - 1, NQG - 1)])
                        W(TE, mV, t_rbs_d[(HPC - 1, NQG - 1)][0])
                        for bb in range(4):
                            W(TE, mA, last_exp_on_bank[bb])
                    for tt in range(8):
                        if gi >= 1 and k == 0:
                            W(TE, mV, t_o_evac.get((gi - 1, tt)))
                        t_g[tt] = OP(TE, lambda e, _th=th, _k=kc_g, _tt=tt, _sl=sl:
                            e.matmul(
                                ptile(_tt),
                                XA[:, _k * T + (_th * 8 + _tt) * 128:
                                   _k * T + (_th * 8 + _tt) * 128 + 128],
                                WR[:, _sl * 512:(_sl + 1) * 512],
                                start=(_k == 0), stop=(_k == CK - 1)),
                            mP if (tt == 7 or k == CK - 1) else None, 1)
                    wtile_owner[sl] = t_g[7]
                for tt in range(8):
                    sl = tt % 4
                    j = gi * 8 + tt
                    W(VE, mP, t_g[tt])
                    WD(VE, tok_xr_g[j])
                    WD(VE, tok_fr_g[j])
                    if tok_out[sl] is not None:
                        WD(VE, tok_out[sl])
                    t1 = OP(VE, lambda e, _tt=tt, _sl=sl: e.tensor_tensor(
                        OUTt(_sl), ptile(_tt), XRt(_sl), ALU.add), mV, 1)
                    W(VE, mV, t1)
                    t_o_evac[(gi, tt)] = OP(VE, lambda e, _sl=sl: e.tensor_tensor(
                        OUTt(_sl), OUTt(_sl), FRt(_sl), ALU.add), mV, 1)
                    xr_owner[sl] = t_o_evac[(gi, tt)]
                    fr_owner[sl] = t_o_evac[(gi, tt)]
                    W(AC, mV, t_o_evac[(gi, tt)])
                    tok_out[sl] = PA.dma(lambda e, _th=th, _d=dch, _tt=tt,
                                         _sl=sl: e.dma_start(
                        out=out[(_th * 8 + _tt) * 128:(_th * 8 + _tt) * 128 + 128,
                                _d * 512:(_d + 1) * 512], in_=OUTt(_sl)))
                    if j + 4 < 64:
                        emit_s4_prefetch(j + 4)

        endmark = PA.mark()
        WD(AC, endmark)
        WD(SY, endmark)

        with nc.Block() as block:
            @block.tensor
            def _(e):
                for f in TE.fns:
                    f(e)

            @block.scalar
            def _(e):
                for f in AC.fns:
                    f(e)

            @block.vector
            def _(e):
                for f in VE.fns:
                    f(e)

            @block.gpsimd
            def _(e):
                for f in GP.fns:
                    f(e)

            @block.sync
            def _(e):
                for f in SY.fns:
                    f(e)

    return nc


def _tile_w_heads(w_slice_T):
    return np.ascontiguousarray(
        w_slice_T.reshape(CK, 128, HPC, 128).transpose(2, 1, 0, 3))


def _tile_w_512(w_T_half):
    return np.ascontiguousarray(
        w_T_half.reshape(CK, 128, 4, 512).transpose(2, 0, 1, 3))


def _host_consts():
    pos = np.arange(T, dtype=np.float64)
    thetas = 1.0 / 10000.0 ** (np.arange(0, HD, 2, dtype=np.float64) / HD)
    pt = np.outer(pos, thetas)
    tab = np.concatenate([pt, pt], axis=1)
    cos = np.cos(tab).astype(np.float32).T
    sin = np.sin(tab).astype(np.float32).T
    sin2 = sin.copy()
    sin2[0:64, :] *= -1.0
    m, p, j = np.meshgrid(np.arange(4), np.arange(128), np.arange(512),
                          indexing="ij")
    masks = ((p + 128 * m) <= j).astype(ml_dtypes.bfloat16)
    consts = np.zeros((128, 8), dtype=np.float32)
    consts[:, 0] = SCALE
    consts[:, 1] = SSHIFT
    consts[:, 2] = DTEMP
    consts[:, 3] = DSHIFT
    consts[:, 4] = EPS
    consts[:, 5] = -0.5
    consts[:, 6] = 1.0 / C
    return (np.ascontiguousarray(cos), np.ascontiguousarray(sin2),
            np.ascontiguousarray(masks), consts)


_NC_CACHE = [None]


def kernel(L0_attn_x, L0_attn_wq, L0_attn_wk, L0_attn_wv, L0_attn_wo,
           L0_ffn_weight, L0_ffn_bias, L0_ffn_centroids, L0_ffn_assignments):
    x = np.asarray(L0_attn_x, dtype=np.float32)
    B = x.shape[0]
    cos, sin2, masks, consts = _host_consts()
    iota = np.arange(64, dtype=np.float32).reshape(64, 1)
    idf = np.eye(128, dtype=np.float32)
    onesf = np.ones((128, 128), dtype=np.float32)
    onesb = np.ones((128, 1), dtype=np.float32)
    cent = np.ascontiguousarray(np.asarray(L0_ffn_centroids, np.float32).T)

    wq_t, wk_t, wv_t, wo_t, wf_t, bias_h, asg_h = {}, {}, {}, {}, {}, {}, {}
    for hg in range(2):
        sl = slice(hg * DH, (hg + 1) * DH)
        wq_t[hg] = _tile_w_heads(np.asarray(L0_attn_wq, np.float32)[sl, :].T)
        wk_t[hg] = _tile_w_heads(np.asarray(L0_attn_wk, np.float32)[sl, :].T)
        wv_t[hg] = _tile_w_heads(np.asarray(L0_attn_wv, np.float32)[sl, :].T)
        wo_t[hg] = _tile_w_512(np.ascontiguousarray(
            np.asarray(L0_attn_wo, np.float32).T[:, sl]))
        wf_t[hg] = _tile_w_512(np.ascontiguousarray(
            np.asarray(L0_ffn_weight, np.float32).T[:, sl]))
        bias_h[hg] = np.ascontiguousarray(
            np.asarray(L0_ffn_bias, np.float32)[sl].reshape(1, DH))
        asg_h[hg] = np.ascontiguousarray(
            np.asarray(L0_ffn_assignments, np.int32)[sl].reshape(1, DH))

    xT_b = [np.ascontiguousarray(x[b].T) for b in range(B)]
    in_maps = []
    for c in range(8):
        b, hg = c // 2, c % 2
        in_maps.append({
            "xT": xT_b[b],
            "xres": np.ascontiguousarray(x[b][:, hg * DH:(hg + 1) * DH]),
            "wq": wq_t[hg], "wk": wk_t[hg], "wv": wv_t[hg],
            "wo": wo_t[hg], "wf": wf_t[hg],
            "cent": cent, "biasd": bias_h[hg], "assignd": asg_h[hg],
            "iota64": iota, "idf": idf, "onesf": onesf, "onesb": onesb,
            "cosT": cos, "sinT2": sin2, "masks": masks, "consts": consts,
        })

    if _NC_CACHE[0] is None:
        _NC_CACHE[0] = build_nc()
    nc = _NC_CACHE[0]
    res = run_bass_kernel_spmd(nc, in_maps, core_ids=list(range(8)))

    outp = np.empty((B, T, C), dtype=np.float32)
    for c in range(8):
        b, hg = c // 2, c % 2
        outp[b][:, hg * DH:(hg + 1) * DH] = res.results[c]["out"]
    return outp


# version 13
# speedup vs baseline: 1.1991x; 1.1991x over previous
import math
import contextlib
import numpy as np
import ml_dtypes
import concourse.bass as bass
import concourse.mybir as mybir
from concourse.bass_utils import run_bass_kernel_spmd

F32 = mybir.dt.float32
BF16 = mybir.dt.bfloat16
I32 = mybir.dt.int32
AF = mybir.ActivationFunctionType
ALU = mybir.AluOpType
AX = mybir.AxisListType

T = 2048
C = 4096
CK = 32
HPC = 16
HD = 128
DH = 2048
NQG = 4
SCALE = 1.0 / math.sqrt(HD)
SSHIFT = -10.0
DTEMP = 10.0
DSHIFT = -30.0
THR = 0.01
EPS = 1e-5


class Sem:
    __slots__ = ("h", "n")

    def __init__(self, h):
        self.h = h
        self.n = 0


class EQ:
    __slots__ = ("name", "fns")

    def __init__(self, name):
        self.name = name
        self.fns = []

    def __call__(self, fn):
        self.fns.append(fn)


def build_nc():
    nc = bass.Bass("TRN2", target_bir_lowering=False)

    dt_in = lambda n, s, d: nc.dram_tensor(n, s, d, kind="ExternalInput").ap()
    xT = dt_in("xT", [C, T], F32)
    xres = dt_in("xres", [T, DH], F32)
    wq = dt_in("wq", [HPC, 128, CK, 128], F32)
    wk = dt_in("wk", [HPC, 128, CK, 128], F32)
    wv = dt_in("wv", [HPC, 128, CK, 128], F32)
    wo = dt_in("wo", [4, CK, 128, 512], F32)
    wf = dt_in("wf", [4, CK, 128, 512], F32)
    cent = dt_in("cent", [C, 64], F32)
    biasd = dt_in("biasd", [1, DH], F32)
    assignd = dt_in("assignd", [1, DH], I32)
    iota64 = dt_in("iota64", [64, 1], F32)
    idf = dt_in("idf", [128, 128], F32)
    onesf = dt_in("onesf", [128, 128], F32)
    onesb = dt_in("onesb", [128, 1], F32)
    cosT = dt_in("cosT", [128, T], F32)
    sinT2 = dt_in("sinT2", [128, T], F32)
    masks = dt_in("masks", [4, 128, 512], BF16)
    consts = dt_in("consts", [128, 8], F32)

    out = nc.dram_tensor("out", [T, DH], F32, kind="ExternalOutput").ap()

    dt_sc = lambda n, s, d: nc.dram_tensor(n, s, d, kind="Internal").ap()
    qts = dt_sc("qts", [HPC, 128, T], BF16)
    kts = dt_sc("kts", [HPC, 128, T], BF16)
    vfs = dt_sc("vfs", [HPC, 128, T], BF16)
    ffng = dt_sc("ffng", [T, DH], F32)
    ccin = dt_sc("ccin", [HPC * 128, T], BF16)
    ccout = dt_sc("ccout", [2 * HPC * 128, T], BF16)
    cmin = dt_sc("cmin", [64, 1], F32)
    cmout = dt_sc("cmout", [64, 1], F32)

    ccin_r = ccin.rearrange("(h p) t -> h p t", p=128)

    es = contextlib.ExitStack()
    sb = lambda n, s, d: es.enter_context(nc.sbuf_tensor(n, s, d))
    with es:
        XA = sb("XA", [128, CK * T], BF16)
        SCRF = sb("SCRF", [128, 6144], F32)
        SCRB = sb("SCRB", [128, 2048], BF16)
        WR = sb("WR", [128, 6144], BF16)
        cosb = sb("cosb", [128, T], F32)
        sinb = sb("sinb", [128, T], F32)
        maskb = sb("maskb", [128, 2048], BF16)
        DOTS = sb("DOTS", [64, 2048], F32)
        constsb = sb("constsb", [128, 8], F32)
        onesfb = sb("onesfb", [128, 128], F32)
        onesbb = sb("onesbb", [128, 1], BF16)
        iotab = sb("iotab", [64, 1], F32)
        idfb = sb("idfb", [128, 128], F32)
        qgcol = sb("qgcol", [128, 16], F32)
        cmcol = sb("cmcol", [64, 1], F32)
        cmgb = sb("cmgb", [64, 1], F32)
        ROWB = sb("ROWB", [128, 4096], BF16)

        psA = es.enter_context(nc.psum_tensor([128, 2048], F32))
        psB = es.enter_context(nc.psum_tensor([128, 2048], F32))

        sm = lambda n: Sem(es.enter_context(nc.semaphore(n)))
        cc = sm("cc")
        mP = sm("mP")
        mA = sm("mA")
        mV = sm("mV")

        TE, AC, VE, GP, SY = EQ("TE"), EQ("AC"), EQ("VE"), EQ("GP"), EQ("SY")
        WAITED = {}

        def W(q, s, v):
            if v is None or v <= 0:
                return
            k = (q.name, id(s))
            if WAITED.get(k, -1) >= v:
                return
            WAITED[k] = v
            q(lambda e, _h=s.h, _v=v: e.wait_ge(_h, _v))

        def OP(q, build, s=None, amt=1):
            if s is None:
                q(build)
                return None
            s.n += amt
            tgt = s.n
            q(lambda e, _b=build, _h=s.h, _a=amt: _b(e).then_inc(_h, _a))
            return tgt

        class LanePool:

            def __init__(self, q, nlanes, name):
                self.q = q
                self.sems = [sm(f"{name}{i}") for i in range(nlanes)]
                self.i = 0

            def dma(self, build):
                s = self.sems[self.i % len(self.sems)]
                self.i += 1
                if s.n > 0:
                    W(self.q, s, s.n)
                tgt = OP(self.q, build, s, 16)
                return (s, tgt)

            def mark(self):
                return [(s, s.n) for s in self.sems if s.n > 0]

        def WD(q, tok):
            if tok is None:
                return
            if isinstance(tok, list):
                for s, v in tok:
                    W(q, s, v)
            else:
                W(q, tok[0], tok[1])

        PG = LanePool(GP, 10, "dG")
        PS = LanePool(SY, 10, "dS")
        PA = LanePool(AC, 8, "dA")

        def xa(k):
            return XA[:, k * T:(k + 1) * T]

        smalls = [
            (cosb[:, :], cosT), (sinb[:, :], sinT2),
            (maskb[:, :].rearrange("p (m j) -> p m j", j=512),
             masks.rearrange("m p j -> p m j")),
            (constsb[:, :], consts), (onesfb[:, :], onesf),
            (iotab[:, :], iota64), (idfb[:, :], idf),
        ]
        tok_small = []
        for o_, i_ in smalls:
            tok_small.append(PS.dma(
                lambda e, _o=o_, _i=i_: e.dma_start(out=_o, in_=_i)))
        tok_onesb = PG.dma(lambda e: e.dma_start(out=onesbb[:, :], in_=onesb))
        tok_bias = PG.dma(lambda e: e.dma_start(out=ROWB[0:1, 0:2048], in_=biasd))
        tok_cent = PG.dma(lambda e: e.dma_start(
            out=WR[:, 0:2048].rearrange("p (k j) -> p k j", j=64),
            in_=cent.rearrange("(k p) j -> p k j", p=128)))
        tok_xa = []
        for k in range(CK):
            tok_xa.append(PG.dma(lambda e, _k=k: e.dma_start(
                out=xa(_k), in_=xT[_k * 128:(_k + 1) * 128, :])))

        WD(AC, tok_onesb)
        t_sq_mm = [None] * CK
        for k in range(CK):
            WD(AC, tok_xa[k])
            if k >= 1:
                W(AC, mP, t_sq_mm[k - 1])
            t_sq = OP(AC, lambda e, _k=k: e.activation(
                SCRB[:, :], xa(_k), AF.Square), mA, 1)
            WD(TE, tok_onesb)
            WD(TE, tok_xa[k])
            for j in range(4):
                OP(TE, lambda e, _k=k, _j=j: e.matmul(
                    psA[0:1, _j * 512:(_j + 1) * 512], onesbb[:, :],
                    xa(_k)[:, _j * 512:(_j + 1) * 512],
                    start=(_k == 0), stop=(_k == CK - 1)))
            W(TE, mA, t_sq)
            for j in range(4):
                t_sq_mm[k] = OP(TE, lambda e, _k=k, _j=j: e.matmul(
                    psB[0:1, _j * 512:(_j + 1) * 512], onesbb[:, :],
                    SCRB[:, _j * 512:(_j + 1) * 512],
                    start=(_k == 0), stop=(_k == CK - 1)),
                    mP if j == 3 else None, 1)
        t_stats = t_sq_mm[CK - 1]

        mean_r = SCRF[0:1, 0:2048]
        rstd_r = SCRF[0:1, 2048:4096]
        tmp_r = SCRF[0:1, 4096:6144]
        u_r = DOTS[0:1, 0:2048]
        W(VE, mP, t_stats)
        t_mean = OP(VE, lambda e: e.tensor_single_scalar(
            mean_r, psA[0:1, :], 1.0 / C, ALU.mult), mV, 1)
        t_msq = OP(VE, lambda e: e.tensor_single_scalar(
            tmp_r, psB[0:1, :], 1.0 / C, ALU.mult), mV, 1)
        W(VE, mV, t_msq)
        t_m2 = OP(VE, lambda e: e.tensor_tensor(
            rstd_r, mean_r, mean_r, ALU.mult), mV, 1)
        W(VE, mV, t_m2)
        t_var = OP(VE, lambda e: e.tensor_tensor(
            tmp_r, tmp_r, rstd_r, ALU.subtract), mV, 1)
        WD(AC, tok_small[3])
        W(AC, mV, t_var)
        t_ln = OP(AC, lambda e: e.activation(
            u_r, tmp_r, AF.Ln, bias=constsb[0:1, 4:5]), mA, 1)
        W(AC, mA, t_ln)
        t_rstd = OP(AC, lambda e: e.activation(
            rstd_r, u_r, AF.Exp, scale=constsb[0:1, 5:6]), mA, 1)

        WD(TE, tok_small[4])
        W(TE, mV, t_msq)
        t_mb = None
        for j in range(4):
            t_mb = OP(TE, lambda e, _j=j: e.matmul(
                psB[:, _j * 512:(_j + 1) * 512], onesfb[0:1, 0:128],
                mean_r[:, _j * 512:(_j + 1) * 512], start=True, stop=True),
                mP if j == 3 else None, 1)
        MBUF = WR[:, 2048:4096]
        W(AC, mP, t_mb)
        t_mbc = OP(AC, lambda e: e.activation(
            MBUF, psB[:, 0:2048], AF.Copy), mA, 1)
        W(TE, mA, t_rstd)
        W(TE, mA, t_mbc)
        t_bcast = None
        for j in range(4):
            t_bcast = OP(TE, lambda e, _j=j: e.matmul(
                psB[:, _j * 512:(_j + 1) * 512], onesfb[0:1, 0:128],
                rstd_r[:, _j * 512:(_j + 1) * 512], start=True, stop=True),
                mP if j == 3 else None, 1)

        tok_xf = [None] * CK
        t_xn = [None] * CK
        t_sub = [None] * CK
        t_dmm = None
        for k in range(CK):
            slot = SCRF[:, (k % 2) * 2048:(k % 2) * 2048 + 2048]
            if k >= 2:
                W(SY, mV, t_xn[k - 2])
            else:
                W(SY, mP, t_bcast)
            tok_xf[k] = PS.dma(lambda e, _k=k, _s=slot: e.dma_start(
                out=_s, in_=xT[_k * 128:(_k + 1) * 128, :]))
            WD(VE, tok_xf[k])
            W(VE, mA, t_mbc)
            W(VE, mP, t_bcast)
            t_sub[k] = OP(VE, lambda e, _s=slot: e.tensor_tensor(
                _s, _s, MBUF, ALU.subtract), mV, 1)
            W(VE, mV, t_sub[k])
            t_xn[k] = OP(VE, lambda e, _k=k, _s=slot: e.tensor_tensor(
                xa(_k), _s, psB[:, 0:2048], ALU.mult), mV, 1)
            W(TE, mV, t_xn[k])
            WD(TE, tok_cent)
            for j in range(4):
                t_dmm = OP(TE, lambda e, _k=k, _j=j: e.matmul(
                    psA[0:64, _j * 512:(_j + 1) * 512],
                    WR[:, _k * 64:(_k + 1) * 64],
                    xa(_k)[:, _j * 512:(_j + 1) * 512],
                    start=(_k == 0), stop=(_k == CK - 1)),
                    mP if (k == CK - 1 and j == 3) else None, 1)

        W(AC, mP, t_dmm)
        t_dexp = OP(AC, lambda e: e.activation(
            DOTS[:, :], psA[0:64, 0:2048], AF.Exp,
            bias=constsb[0:64, 3:4], scale=constsb[0:64, 2:3]), mA, 1)
        W(TE, mA, t_dexp)
        W(TE, mV, t_xn[CK - 1])
        t_dsum = None
        for j in range(4):
            t_dsum = OP(TE, lambda e, _j=j: e.matmul(
                psB[0:1, _j * 512:(_j + 1) * 512], onesfb[0:64, 0:1],
                DOTS[:, _j * 512:(_j + 1) * 512], start=True, stop=True),
                mP if j == 3 else None, 1)
        W(VE, mP, t_dsum)
        t_thr = OP(VE, lambda e: e.tensor_single_scalar(
            tmp_r, psB[0:1, :], THR, ALU.mult), mV, 1)
        W(TE, mV, t_thr)
        t_thb = None
        for j in range(4):
            t_thb = OP(TE, lambda e, _j=j: e.matmul(
                psB[0:64, _j * 512:(_j + 1) * 512], onesfb[0:1, 0:64],
                tmp_r[:, _j * 512:(_j + 1) * 512], start=True, stop=True),
                mP if j == 3 else None, 1)
        W(VE, mP, t_thb)
        t_diff = OP(VE, lambda e: e.tensor_tensor(
            DOTS[:, :], DOTS[:, :], psB[0:64, 0:2048], ALU.subtract), mV, 1)
        W(VE, mV, t_diff)
        t_cm = OP(VE, lambda e: e.reduce_max(
            cmcol[:, :], DOTS[:, :], axis=AX.X), mV, 1)
        W(SY, mV, t_cm)
        tok_cmw = PS.dma(lambda e: e.dma_start(out=cmin, in_=cmcol[:, :]))
        WD(GP, tok_cmw)
        t_cc1 = OP(GP, lambda e: e.collective_compute(
            "AllReduce", ALU.max, replica_groups=[list(range(8))],
            ins=[cmin.opt()], outs=[cmout.opt()]), cc, 1)

        WD(TE, tok_small[6])
        t_qtr = [None] * 16
        for tt in range(16):
            W(TE, mV, t_cm)
            t_qtr[tt] = OP(TE, lambda e, _tt=tt: e.transpose(
                psA[:, _tt * 64:(_tt + 1) * 64],
                DOTS[:, _tt * 128:(_tt + 1) * 128], idfb[0:64, 0:64]), mP, 1)
        t_qg = [None] * 16
        for tt in range(16):
            W(VE, mP, t_qtr[15])
            t1 = OP(VE, lambda e, _tt=tt: e.reduce_max(
                qgcol[:, _tt:_tt + 1], psA[:, _tt * 64:(_tt + 1) * 64],
                axis=AX.X), mV, 1)
            W(VE, mV, t1)
            t_qg[tt] = OP(VE, lambda e, _tt=tt: e.tensor_single_scalar(
                qgcol[:, _tt:_tt + 1], qgcol[:, _tt:_tt + 1], 0.0, ALU.is_gt),
                mV, 1)
        t_qr = None
        for tt in range(16):
            W(TE, mV, t_qg[tt])
            t_qr = OP(TE, lambda e, _tt=tt: e.matmul(
                psB[0:1, _tt * 128:(_tt + 1) * 128], qgcol[:, _tt:_tt + 1],
                idfb[:, :], start=True, stop=True),
                mP if tt == 15 else None, 1)
        W(AC, mP, t_qr)
        t_qrow = OP(AC, lambda e: e.activation(
            ROWB[0:1, 2048:4096], psB[0:1, 0:2048], AF.Copy), mA, 1)

        QF = SCRF[:, 0:2048]
        ROT = SCRF[:, 2048:4096]
        ROT2 = SCRF[:, 4096:6144]
        QRO = SCRB[:, 0:2048]

        mats = []
        for h in range(HPC):
            mats.append(("q", h, wq, qts))
            mats.append(("k", h, wk, kts))
            mats.append(("v", h, wv, vfs))

        wslot_owner = [None, None, None]
        t_mat = [None] * len(mats)
        t_rope = [None] * len(mats)
        t_ract = [None] * len(mats)
        tok_spill = [None] * len(mats)
        t_rot_read = 0
        t_qf_read = 0
        for mi, (kind, h, wsrc, dst) in enumerate(mats):
            tok_wh = [None, None]
            for half in range(2):
                sl = (2 * mi + half) % 3
                if wslot_owner[sl] is not None:
                    W(GP, mP, wslot_owner[sl])
                tok_wh[half] = PG.dma(lambda e, _w=wsrc, _h=h, _hf=half, _sl=sl:
                    e.dma_start(
                        out=WR[:, _sl * 2048:(_sl + 1) * 2048].rearrange(
                            "p (k j) -> p k j", j=128),
                        in_=_w[_h][:, _hf * 16:(_hf + 1) * 16, :]))

            psP = psA if (mi % 2 == 0) else psB
            if mi == 0:
                W(TE, mV, t_qg[15])
            elif mi == 1:
                W(TE, mA, t_qrow)
                W(TE, mV, t_diff)
            else:
                W(TE, mV, t_rope[mi - 2])
                W(TE, mA, t_ract[mi - 2])
            t_half_done = [None, None]
            for k in range(CK):
                half = k // 16
                sl = (2 * mi + half) % 3
                WD(TE, tok_wh[half])
                for j in range(4):
                    inc = (j == 3 and (k == 15 or k == CK - 1))
                    t = OP(TE, lambda e, _p=psP, _k=k, _j=j, _sl=sl:
                        e.matmul(
                            _p[:, _j * 512:(_j + 1) * 512],
                            WR[:, _sl * 2048 + (_k % 16) * 128:
                               _sl * 2048 + (_k % 16) * 128 + 128],
                            xa(_k)[:, _j * 512:(_j + 1) * 512],
                            start=(_k == 0), stop=(_k == CK - 1)),
                        mP if inc else None, 1)
                    if inc:
                        t_half_done[half] = t
            t_mat[mi] = t_half_done[1]
            wslot_owner[(2 * mi) % 3] = t_half_done[0]
            wslot_owner[(2 * mi + 1) % 3] = t_half_done[1]

            if kind == "v":
                W(AC, mP, t_mat[mi])
                if mi >= 1:
                    WD(AC, tok_spill[mi - 1])
                t_ract[mi] = OP(AC, lambda e, _p=psP: e.activation(
                    QRO, _p[:, 0:2048], AF.Copy), mA, 1)
                t_rope[mi] = mV.n
                W(SY, mA, t_ract[mi])
                tok_spill[mi] = PS.dma(lambda e, _h=h, _d=dst: e.dma_start(
                    out=_d[_h], in_=QRO))
            else:
                W(AC, mP, t_mat[mi])
                W(AC, mV, t_rot_read)
                OP(AC, lambda e, _p=psP: e.activation(
                    ROT[0:64, :], _p[64:128, 0:2048], AF.Copy))
                t_ract[mi] = OP(AC, lambda e, _p=psP: e.activation(
                    ROT[64:128, :], _p[0:64, 0:2048], AF.Copy), mA, 1)
                WD(VE, tok_small[0])
                WD(VE, tok_small[1])
                W(VE, mP, t_mat[mi])
                W(VE, mV, t_qf_read)
                W(VE, mA, t_ract[mi])
                t1 = OP(VE, lambda e, _p=psP: e.tensor_tensor(
                    QF, _p[:, 0:2048], cosb[:, :], ALU.mult), mV, 1)
                t_rope[mi] = t1
                W(VE, mA, t_ract[mi])
                t2 = OP(VE, lambda e: e.tensor_tensor(
                    ROT2, ROT, sinb[:, :], ALU.mult), mV, 1)
                t_rot_read = t2
                W(VE, mV, t2)
                if mi >= 1:
                    WD(VE, tok_spill[mi - 1])
                t3 = OP(VE, lambda e: e.tensor_tensor(
                    QRO, QF, ROT2, ALU.add), mV, 1)
                t_qf_read = t3
                W(SY, mV, t3)
                tok_spill[mi] = PS.dma(lambda e, _h=h, _d=dst: e.dma_start(
                    out=_d[_h], in_=QRO))

        W(SY, cc, t_cc1)
        tok_cmr = PS.dma(lambda e: e.dma_start(out=cmgb[:, :], in_=cmout))
        assignf = SCRF[0:1, 0:2048]
        WD(GP, tok_spill[-2])
        tok_asf = PG.dma(lambda e: e.dma_start(out=assignf, in_=assignd))
        WD(TE, tok_asf)
        W(TE, mA, t_ract[-1])
        W(TE, mV, t_rope[-1] or 0)
        t_ab = None
        for j in range(4):
            t_ab = OP(TE, lambda e, _j=j: e.matmul(
                psB[0:64, _j * 512:(_j + 1) * 512], onesfb[0:1, 0:64],
                assignf[:, _j * 512:(_j + 1) * 512], start=True, stop=True),
                mP if j == 3 else None, 1)
        WD(VE, tok_small[5])
        W(VE, mP, t_ab)
        onehot = SCRF[0:64, 0:2048]
        t_oh = OP(VE, lambda e: e.tensor_scalar(
            onehot, psB[0:64, 0:2048], iotab[:, :], None, ALU.is_equal), mV, 1)
        W(TE, mV, t_oh)
        WD(TE, tok_cmr)
        t_rgm = None
        for j in range(4):
            t_rgm = OP(TE, lambda e, _j=j: e.matmul(
                psB[0:1, _j * 512:(_j + 1) * 512], cmgb[:, :],
                onehot[:, _j * 512:(_j + 1) * 512], start=True, stop=True),
                mP if j == 3 else None, 1)
        W(VE, mP, t_rgm)
        t_rg01 = OP(VE, lambda e: e.tensor_single_scalar(
            tmp_r, psB[0:1, :], 0.0, ALU.is_gt), mV, 1)
        W(TE, mV, t_rg01)
        t_rgb = None
        for j in range(4):
            t_rgb = OP(TE, lambda e, _j=j: e.matmul(
                psB[:, _j * 512:(_j + 1) * 512], onesfb[0:1, 0:128],
                tmp_r[:, _j * 512:(_j + 1) * 512], start=True, stop=True),
                mP if j == 3 else None, 1)
        W(AC, mP, t_rgb)
        rgb = SCRF[:, 0:2048]
        t_rgbc = OP(AC, lambda e: e.activation(
            rgb, psB[:, 0:2048], AF.Copy), mA, 1)

        def ptile(tt):
            return (psA if tt < 4 else psB)[:, (tt % 4) * 512:(tt % 4) * 512 + 512]

        wtile_owner = [t_mat[-1]] * 12
        tok_stage = [None, None]
        t_ffn_evac = {}
        t_ffn_xa_low = None
        t_ffn_xa_low2 = None
        for th in range(2):
            for dch in range(4):
                gi = th * 4 + dch
                t_g = [None] * 8
                for k in range(CK):
                    sl = (gi * CK + k) % 12
                    if wtile_owner[sl] is not None:
                        W(GP, mP, wtile_owner[sl])
                    tok_wt = PG.dma(lambda e, _d=dch, _k=k, _sl=sl:
                        e.dma_start(out=WR[:, _sl * 512:(_sl + 1) * 512],
                                    in_=wf[_d, _k]))
                    WD(TE, tok_wt)
                    if gi == 0 and k == 0:
                        W(TE, mA, t_rgbc)
                    for tt in range(8):
                        if gi >= 1 and k == 0:
                            W(TE, mV, t_ffn_evac.get((gi - 1, tt)))
                        t_g[tt] = OP(TE, lambda e, _th=th, _k=k, _tt=tt, _sl=sl:
                            e.matmul(
                                ptile(_tt),
                                XA[:, _k * T + (_th * 8 + _tt) * 128:
                                   _k * T + (_th * 8 + _tt) * 128 + 128],
                                WR[:, _sl * 512:(_sl + 1) * 512],
                                start=(_k == 0), stop=False),
                            mP if tt == 7 else None, 1)
                    wtile_owner[sl] = t_g[7]
                    if gi == 7 and k == 23:
                        t_ffn_xa_low = t_g[7]
                    if gi == 7 and k == 26:
                        t_ffn_xa_low2 = t_g[7]
                W(TE, mA, t_qrow)
                WD(TE, tok_bias)
                for tt in range(8):
                    t_g[tt] = OP(TE, lambda e, _th=th, _d=dch, _tt=tt: e.matmul(
                        ptile(_tt),
                        ROWB[0:1, 2048 + (_th * 8 + _tt) * 128:
                             2048 + (_th * 8 + _tt) * 128 + 128],
                        ROWB[0:1, _d * 512:(_d + 1) * 512],
                        start=False, stop=True), mP, 1)
                for tt in range(8):
                    ss = tt % 2
                    stage = SCRF[:, 2048 + ss * 512:2048 + ss * 512 + 512]
                    W(VE, mP, t_g[tt])
                    W(VE, mA, t_rgbc)
                    WD(VE, tok_stage[ss])
                    t_ffn_evac[(gi, tt)] = OP(VE, lambda e, _th=th, _d=dch,
                                              _tt=tt, _st=stage:
                        e.scalar_tensor_tensor(
                            _st, ptile(_tt),
                            qgcol[:, _th * 8 + _tt:_th * 8 + _tt + 1],
                            rgb[:, _d * 512:(_d + 1) * 512],
                            ALU.mult, ALU.mult), mV, 1)
                    W(AC, mV, t_ffn_evac[(gi, tt)])
                    tok_stage[ss] = PA.dma(lambda e, _th=th, _d=dch, _tt=tt,
                                           _st=stage: e.dma_start(
                        out=ffng[(_th * 8 + _tt) * 128:(_th * 8 + _tt) * 128 + 128,
                                 _d * 512:(_d + 1) * 512], in_=_st))
        tok_ffn_done = PA.mark()
        t_ffn_pe = mP.n
        t_ffn_ve = t_ffn_evac[(7, 7)]

        AB = 43008
        def hslot(h, which):
            base = AB + (h % 2) * 6144
            off = {"q": 0, "k": 2048, "v": 4096}[which]
            return XA[:, base + off:base + off + 2048]

        def pslot(kc):
            return XA[:, AB + 12288 + kc * 512:AB + 12288 + kc * 512 + 512]

        def aoslot(par):
            return XA[:, AB + 20480 + par * 512:AB + 20480 + par * 512 + 512]

        tok_head_load = [None] * HPC
        t_head_pe = [None] * HPC
        t_exp = {}
        t_msk = {}
        t_avdone = {}
        t_csread = {}
        t_rbs_d = {}
        t_rc_d = {}
        tok_aow = {}
        last_exp_on_bank = [None] * 4
        last_av_on_slot = [None] * 16
        t_rbc_last = None
        recrow = lambda par: SCRF[0:1, 3072 + par * 512:3072 + par * 512 + 512]
        rbs = lambda par: SCRF[:, 4096 + par * 512:4096 + par * 512 + 512]

        def emit_head_load(h):
            WD(SY, tok_spill[3 * h])
            WD(SY, tok_spill[3 * h + 1])
            WD(SY, tok_spill[3 * h + 2])
            W(SY, mP, t_head_pe[h - 2] if h >= 2 else
              (t_ffn_xa_low if h == 0 else t_ffn_xa_low2))
            PS.dma(lambda e, _h=h: e.dma_start(out=hslot(_h, "q"), in_=qts[_h]))
            PS.dma(lambda e, _h=h: e.dma_start(out=hslot(_h, "k"), in_=kts[_h]))
            for kt in range(16):
                PS.dma(lambda e, _h=h, _kt=kt: e.dma_start(
                    out=hslot(_h, "v")[:, _kt * 128:(_kt + 1) * 128],
                    in_=vfs[_h][:, _kt * 128:(_kt + 1) * 128],
                    transpose=True))
            tok_head_load[h] = PS.mark()

        W(TE, mV, t_ffn_ve)
        pend_rbc = [None]
        tok_aT_chunk = {}
        AG_START = [0, 6, 12, 15]
        AG_NH = [6, 6, 3, 1]
        AG_OUT = [0, 1536, 3072, 3840]
        AT_EARLY = [(0, 0, 0, 6), (1, 0, 6, 6), (2, 0, 12, 3), (3, 0, 15, 1),
                    (0, 1, 16, 5)]
        AT_LATE = [(0, 1, 21, 1), (1, 1, 22, 6), (2, 1, 28, 3), (3, 1, 31, 1)]

        def emit_ag(i):
            if pend_rbc[0] is not None:
                pend_rbc[0]()
                pend_rbc[0] = None
            for s_, v_ in PS.mark():
                W(GP, s_, v_)
            W(GP, cc, 1 + i)
            if i == 0:
                W(GP, mP, t_ffn_pe)
            OP(GP, lambda e, _i=i: e.collective_compute(
                "AllGather", ALU.bypass,
                replica_groups=[[0, 1], [2, 3], [4, 5], [6, 7]],
                ins=[ccin[AG_START[_i] * 128:(AG_START[_i] + AG_NH[_i]) * 128,
                          :].opt()],
                outs=[ccout[AG_OUT[_i]:AG_OUT[_i] + 2 * AG_NH[_i] * 128,
                            :].opt()]), cc, 1)
            for (ii, r, g0, nh) in AT_EARLY:
                if ii != i:
                    continue
                W(GP, cc, 2 + i)
                ro = (AG_OUT[ii] + r * AG_NH[ii] * 128
                      + (g0 - r * 16 - AG_START[ii]) * 128)
                tok = PG.dma(lambda e, _ro=ro, _g0=g0, _nh=nh:
                    e.dma_start(
                        out=XA[:, _g0 * 2048:(_g0 + _nh) * 2048].rearrange(
                            "p (h t) -> p h t", t=T),
                        in_=ccout[_ro:_ro + _nh * 128, :].rearrange(
                            "(h p) t -> p h t", p=128)))
                for g in range(g0, g0 + nh):
                    tok_aT_chunk[g] = tok

        W(TE, mV, t_ffn_ve)
        emit_head_load(0)
        for h in range(HPC):
            if h + 1 < HPC:
                emit_head_load(h + 1)
            qb, kb, vb = hslot(h, "q"), hslot(h, "k"), hslot(h, "v")
            t_sc = {}

            def AVP(qg):
                par = qg % 2
                return psB[:, (0 if par == 0 else 3) * 512:
                           (0 if par == 0 else 3) * 512 + 512]

            CSP = psB[0:1, 512:1024]

            def emit_sc(qg, kc, _h=h, _qb=qb, _kb=kb):
                WD(TE, tok_head_load[_h])
                W(TE, mA, last_exp_on_bank[kc % 4])
                t_sc[(qg, kc)] = OP(TE, lambda e, _kc=kc, _qg=qg: e.matmul(
                    psA[:, (_kc % 4) * 512:(_kc % 4) * 512 + 512],
                    _kb[:, _kc * 128:(_kc + 1) * 128],
                    _qb[:, _qg * 512:(_qg + 1) * 512],
                    start=True, stop=True), mP, 1)

            def emit_exp(qg, kc, _h=h):
                W(AC, mP, t_sc[(qg, kc)])
                W(AC, mP, last_av_on_slot[kc])
                t_exp[(_h, qg, kc)] = OP(AC, lambda e, _kc=kc: e.activation(
                    pslot(_kc), psA[:, (_kc % 4) * 512:(_kc % 4) * 512 + 512],
                    AF.Exp, bias=constsb[:, 1:2], scale=constsb[:, 0:1]),
                    mA, 1)
                last_exp_on_bank[kc % 4] = t_exp[(_h, qg, kc)]
                if kc >= 4 * qg:
                    m = kc - 4 * qg
                    W(VE, mA, t_exp[(_h, qg, kc)])
                    t_msk[(_h, qg, kc)] = OP(VE, lambda e, _kc=kc, _m=m:
                        e.tensor_tensor(
                            pslot(_kc), pslot(_kc),
                            maskb[:, _m * 512:(_m + 1) * 512], ALU.mult),
                        mV, 1)

            def emit_cs(qg, kc, start, _h=h):
                kcn = 4 * (qg + 1)
                if start:
                    pr = (_h, qg - 1) if qg >= 1 else (
                        (_h - 1, 3) if _h >= 1 else None)
                    if pr is not None and pr in t_rc_d:
                        W(TE, mV, t_rc_d[pr])
                return OP(TE, lambda e, _kc=kc, _st=start: e.matmul(
                    CSP, onesbb[:, :], pslot(_kc),
                    start=_st, stop=(_kc == kcn - 1)), mP, 1)

            def emit_csav(qg, kc, _h=h, _vb=vb):
                kcn = 4 * (qg + 1)
                avp = AVP(qg)
                if (_h, qg, kc) in t_msk:
                    W(TE, mV, t_msk[(_h, qg, kc)])
                else:
                    W(TE, mA, t_exp[(_h, qg, kc)])
                if kc == 0:
                    pk = (_h, qg - 2) if qg >= 2 else (
                        (_h - 1, qg + 2) if _h >= 1 else None)
                    if pk is not None:
                        W(TE, mV, t_avdone.get(pk))
                if kc == 1:
                    emit_cs(qg, 1, True)
                    last_av_on_slot[0] = emit_cs(qg, 0, False)
                elif kc >= 2 or kcn == 1:
                    emit_cs(qg, kc, False)
                t = OP(TE, lambda e, _kc=kc, _kcn=kcn, _avp=avp: e.matmul(
                    _avp, _vb[:, _kc * 128:(_kc + 1) * 128], pslot(_kc),
                    start=(_kc == 0), stop=(_kc == _kcn - 1)), mP, 1)
                last_av_on_slot[kc] = t
                return t

            for j in range(4):
                emit_sc(0, j)
            for qg in range(NQG):
                kcn = 4 * (qg + 1)
                par = qg % 2
                avp = AVP(qg)
                t_last_av = None
                for kc in range(kcn):
                    emit_exp(qg, kc)
                    t_last_av = emit_csav(qg, kc)
                    if kc == 1 and pend_rbc[0] is not None:
                        t_rbc_last = pend_rbc[0]()
                        pend_rbc[0] = None
                    nxt = kc + 4
                    if nxt < kcn:
                        emit_sc(qg, nxt)
                    elif qg + 1 < NQG and nxt - kcn < 4:
                        emit_sc(qg + 1, nxt - kcn)
                t_csread[(h, qg)] = t_last_av
                t_head_pe[h] = t_last_av

                W(VE, mP, t_last_av)
                t_rc = OP(VE, lambda e, _p=par: e.reciprocal(
                    recrow(_p), CSP), mV, 1)
                t_rc_d[(h, qg)] = t_rc

                def emit_rbc(_h=h, _qg=qg, _par=par, _avp=avp, _trc=t_rc):
                    W(TE, mV, _trc)
                    prevq = (_h, _qg - 1) if _qg >= 1 else (
                        (_h - 1, 3) if _h >= 1 else None)
                    if prevq is not None and prevq in t_rbs_d:
                        W(TE, mV, t_rbs_d[prevq][0])
                    t_rbc = OP(TE, lambda e, _p=_par: e.matmul(
                        psB[:, 1024:1536], onesfb[0:1, 0:128], recrow(_p),
                        start=True, stop=True), mP, 1)
                    pk2 = (_h, _qg - 2) if _qg >= 2 else (
                        (_h - 1, _qg + 2) if _h >= 1 else None)
                    W(VE, mP, t_rbc)
                    if pk2 is not None:
                        W(VE, mV, t_avdone.get(pk2))
                    t_rbs = OP(VE, lambda e, _p=_par: e.tensor_copy(
                        rbs(_p), psB[:, 1024:1536]), mV, 1)
                    t_rbs_d[(_h, _qg)] = (t_rbs, t_rbc, t_rc)
                    W(VE, mV, t_rbs)
                    if pk2 is not None:
                        WD(VE, tok_aow.get(pk2))
                    t_av = OP(VE, lambda e, _p=_par, _a=_avp: e.tensor_tensor(
                        aoslot(_p), _a, rbs(_p), ALU.mult), mV, 1)
                    t_avdone[(_h, _qg)] = t_av
                    W(SY, mV, t_av)
                    tok_aow[(_h, _qg)] = PS.dma(lambda e, _hh=_h, _qq=_qg,
                                                _p=_par: e.dma_start(
                        out=ccin_r[_hh][:, _qq * 512:(_qq + 1) * 512],
                        in_=aoslot(_p)))
                    return t_rbc

                pend_rbc[0] = emit_rbc
                if qg == NQG - 1 and h in (5, 11, 14, 15):
                    emit_ag((5, 11, 14, 15).index(h))

        if pend_rbc[0] is not None:
            t_rbc_last = pend_rbc[0]()
            pend_rbc[0] = None

        W(GP, mP, t_head_pe[HPC - 1])
        W(GP, mV, t_avdone[(HPC - 1, NQG - 1)])
        WD(GP, tok_aow.get((HPC - 1, NQG - 1)))
        WD(GP, tok_aow.get((HPC - 1, NQG - 2)))
        for (i, r, g0, nh) in AT_LATE:
            W(GP, cc, 2 + i)
            ro = (AG_OUT[i] + r * AG_NH[i] * 128
                  + (g0 - r * 16 - AG_START[i]) * 128)
            tok = PG.dma(lambda e, _ro=ro, _g0=g0, _nh=nh:
                e.dma_start(
                    out=XA[:, _g0 * 2048:(_g0 + _nh) * 2048].rearrange(
                        "p (h t) -> p h t", t=T),
                    in_=ccout[_ro:_ro + _nh * 128, :].rearrange(
                        "(h p) t -> p h t", p=128)))
            for g in range(g0, g0 + nh):
                tok_aT_chunk[g] = tok

        KORD = [0, 1, 2, 3, 4, 5, 16, 17, 18, 19, 20, 21,
                6, 7, 8, 9, 10, 11, 22, 23, 24, 25, 26, 27,
                12, 13, 14, 28, 29, 30, 15, 31]
        XRt = lambda s: SCRF[:, s * 512:s * 512 + 512]
        FRt = lambda s: SCRF[:, 2048 + s * 512:2048 + s * 512 + 512]
        OUTt = lambda s: SCRF[:, 4096 + s * 512:4096 + s * 512 + 512]
        xr_owner = [None] * 4
        fr_owner = [None] * 4
        tok_out = [None] * 4
        t_o_evac = {}
        wtile_owner = [None] * 12
        tok_xr_g = {}
        tok_fr_g = {}

        def emit_s4_prefetch(j):
            gj, ttj = j // 8, j % 8
            thj, dchj = gj // 4, gj % 4
            sl = ttj % 4
            row = (thj * 8 + ttj) * 128
            col = dchj * 512
            if xr_owner[sl] is not None:
                W(AC, mV, xr_owner[sl])
            tok_xr_g[j] = PA.dma(lambda e, _r=row, _c=col, _sl=sl: e.dma_start(
                out=XRt(_sl), in_=xres[_r:_r + 128, _c:_c + 512]))
            if fr_owner[sl] is not None:
                W(AC, mV, fr_owner[sl])
            WD(AC, tok_ffn_done)
            tok_fr_g[j] = PA.dma(lambda e, _r=row, _c=col, _sl=sl: e.dma_start(
                out=FRt(_sl), in_=ffng[_r:_r + 128, _c:_c + 512]))

        W(AC, mV, t_avdone[(HPC - 1, NQG - 1)])
        W(AC, mP, t_rbc_last)
        for j in range(4):
            emit_s4_prefetch(j)
        for th in range(2):
            for dch in range(4):
                gi = th * 4 + dch
                t_g = [None] * 8
                for k in range(CK):
                    kc_g = KORD[k]
                    sl = (gi * CK + k) % 12
                    if wtile_owner[sl] is not None:
                        W(GP, mP, wtile_owner[sl])
                    tok_wt = PG.dma(lambda e, _d=dch, _k=kc_g, _sl=sl:
                        e.dma_start(out=WR[:, _sl * 512:(_sl + 1) * 512],
                                    in_=wo[_d, _k]))
                    WD(TE, tok_wt)
                    WD(TE, tok_aT_chunk[kc_g])
                    if gi == 0 and k == 0:
                        W(TE, mV, t_avdone[(HPC - 1, NQG - 1)])
                        W(TE, mV, t_rbs_d[(HPC - 1, NQG - 1)][0])
                        for bb in range(4):
                            W(TE, mA, last_exp_on_bank[bb])
                    for tt in range(8):
                        if gi >= 1 and k == 0:
                            W(TE, mV, t_o_evac.get((gi - 1, tt)))
                        t_g[tt] = OP(TE, lambda e, _th=th, _k=kc_g, _tt=tt, _sl=sl:
                            e.matmul(
                                ptile(_tt),
                                XA[:, _k * T + (_th * 8 + _tt) * 128:
                                   _k * T + (_th * 8 + _tt) * 128 + 128],
                                WR[:, _sl * 512:(_sl + 1) * 512],
                                start=(_k == 0), stop=(_k == CK - 1)),
                            mP if (tt == 7 or k == CK - 1) else None, 1)
                    wtile_owner[sl] = t_g[7]
                for tt in range(8):
                    sl = tt % 4
                    j = gi * 8 + tt
                    W(VE, mP, t_g[tt])
                    WD(VE, tok_xr_g[j])
                    WD(VE, tok_fr_g[j])
                    if tok_out[sl] is not None:
                        WD(VE, tok_out[sl])
                    t1 = OP(VE, lambda e, _tt=tt, _sl=sl: e.tensor_tensor(
                        OUTt(_sl), ptile(_tt), XRt(_sl), ALU.add), mV, 1)
                    W(VE, mV, t1)
                    t_o_evac[(gi, tt)] = OP(VE, lambda e, _sl=sl: e.tensor_tensor(
                        OUTt(_sl), OUTt(_sl), FRt(_sl), ALU.add), mV, 1)
                    xr_owner[sl] = t_o_evac[(gi, tt)]
                    fr_owner[sl] = t_o_evac[(gi, tt)]
                    W(AC, mV, t_o_evac[(gi, tt)])
                    tok_out[sl] = PA.dma(lambda e, _th=th, _d=dch, _tt=tt,
                                         _sl=sl: e.dma_start(
                        out=out[(_th * 8 + _tt) * 128:(_th * 8 + _tt) * 128 + 128,
                                _d * 512:(_d + 1) * 512], in_=OUTt(_sl)))
                    if j + 4 < 64:
                        emit_s4_prefetch(j + 4)

        endmark = PA.mark()
        WD(AC, endmark)
        WD(SY, endmark)

        with nc.Block() as block:
            @block.tensor
            def _(e):
                for f in TE.fns:
                    f(e)

            @block.scalar
            def _(e):
                for f in AC.fns:
                    f(e)

            @block.vector
            def _(e):
                for f in VE.fns:
                    f(e)

            @block.gpsimd
            def _(e):
                for f in GP.fns:
                    f(e)

            @block.sync
            def _(e):
                for f in SY.fns:
                    f(e)

    return nc


def _tile_w_heads(w_slice_T):
    return np.ascontiguousarray(
        w_slice_T.reshape(CK, 128, HPC, 128).transpose(2, 1, 0, 3))


def _tile_w_512(w_T_half):
    return np.ascontiguousarray(
        w_T_half.reshape(CK, 128, 4, 512).transpose(2, 0, 1, 3))


def _host_consts():
    pos = np.arange(T, dtype=np.float64)
    thetas = 1.0 / 10000.0 ** (np.arange(0, HD, 2, dtype=np.float64) / HD)
    pt = np.outer(pos, thetas)
    tab = np.concatenate([pt, pt], axis=1)
    cos = np.cos(tab).astype(np.float32).T
    sin = np.sin(tab).astype(np.float32).T
    sin2 = sin.copy()
    sin2[0:64, :] *= -1.0
    m, p, j = np.meshgrid(np.arange(4), np.arange(128), np.arange(512),
                          indexing="ij")
    masks = ((p + 128 * m) <= j).astype(ml_dtypes.bfloat16)
    consts = np.zeros((128, 8), dtype=np.float32)
    consts[:, 0] = SCALE
    consts[:, 1] = SSHIFT
    consts[:, 2] = DTEMP
    consts[:, 3] = DSHIFT
    consts[:, 4] = EPS
    consts[:, 5] = -0.5
    consts[:, 6] = 1.0 / C
    return (np.ascontiguousarray(cos), np.ascontiguousarray(sin2),
            np.ascontiguousarray(masks), consts)


_NC_CACHE = [None]


def kernel(L0_attn_x, L0_attn_wq, L0_attn_wk, L0_attn_wv, L0_attn_wo,
           L0_ffn_weight, L0_ffn_bias, L0_ffn_centroids, L0_ffn_assignments):
    x = np.asarray(L0_attn_x, dtype=np.float32)
    B = x.shape[0]
    cos, sin2, masks, consts = _host_consts()
    iota = np.arange(64, dtype=np.float32).reshape(64, 1)
    idf = np.eye(128, dtype=np.float32)
    onesf = np.ones((128, 128), dtype=np.float32)
    onesb = np.ones((128, 1), dtype=np.float32)
    cent = np.ascontiguousarray(np.asarray(L0_ffn_centroids, np.float32).T)

    wq_t, wk_t, wv_t, wo_t, wf_t, bias_h, asg_h = {}, {}, {}, {}, {}, {}, {}
    for hg in range(2):
        sl = slice(hg * DH, (hg + 1) * DH)
        wq_t[hg] = _tile_w_heads(np.asarray(L0_attn_wq, np.float32)[sl, :].T)
        wk_t[hg] = _tile_w_heads(np.asarray(L0_attn_wk, np.float32)[sl, :].T)
        wv_t[hg] = _tile_w_heads(np.asarray(L0_attn_wv, np.float32)[sl, :].T)
        wo_t[hg] = _tile_w_512(np.ascontiguousarray(
            np.asarray(L0_attn_wo, np.float32).T[:, sl]))
        wf_t[hg] = _tile_w_512(np.ascontiguousarray(
            np.asarray(L0_ffn_weight, np.float32).T[:, sl]))
        bias_h[hg] = np.ascontiguousarray(
            np.asarray(L0_ffn_bias, np.float32)[sl].reshape(1, DH))
        asg_h[hg] = np.ascontiguousarray(
            np.asarray(L0_ffn_assignments, np.int32)[sl].reshape(1, DH))

    xT_b = [np.ascontiguousarray(x[b].T) for b in range(B)]
    in_maps = []
    for c in range(8):
        b, hg = c // 2, c % 2
        in_maps.append({
            "xT": xT_b[b],
            "xres": np.ascontiguousarray(x[b][:, hg * DH:(hg + 1) * DH]),
            "wq": wq_t[hg], "wk": wk_t[hg], "wv": wv_t[hg],
            "wo": wo_t[hg], "wf": wf_t[hg],
            "cent": cent, "biasd": bias_h[hg], "assignd": asg_h[hg],
            "iota64": iota, "idf": idf, "onesf": onesf, "onesb": onesb,
            "cosT": cos, "sinT2": sin2, "masks": masks, "consts": consts,
        })

    if _NC_CACHE[0] is None:
        _NC_CACHE[0] = build_nc()
    nc = _NC_CACHE[0]
    res = run_bass_kernel_spmd(nc, in_maps, core_ids=list(range(8)))

    outp = np.empty((B, T, C), dtype=np.float32)
    for c in range(8):
        b, hg = c // 2, c % 2
        outp[b][:, hg * DH:(hg + 1) * DH] = res.results[c]["out"]
    return outp


# version 14
# speedup vs baseline: 1.2032x; 1.0035x over previous
import math
import contextlib
import numpy as np
import ml_dtypes
import concourse.bass as bass
import concourse.mybir as mybir
from concourse.bass_utils import run_bass_kernel_spmd

F32 = mybir.dt.float32
BF16 = mybir.dt.bfloat16
I32 = mybir.dt.int32
AF = mybir.ActivationFunctionType
ALU = mybir.AluOpType
AX = mybir.AxisListType

T = 2048
C = 4096
CK = 32
HPC = 16
HD = 128
DH = 2048
NQG = 4
SCALE = 1.0 / math.sqrt(HD)
SSHIFT = -10.0
DTEMP = 10.0
DSHIFT = -30.0
THR = 0.01
EPS = 1e-5


class Sem:
    __slots__ = ("h", "n")

    def __init__(self, h):
        self.h = h
        self.n = 0


class EQ:
    __slots__ = ("name", "fns")

    def __init__(self, name):
        self.name = name
        self.fns = []

    def __call__(self, fn):
        self.fns.append(fn)


def build_nc():
    nc = bass.Bass("TRN2", target_bir_lowering=False)

    dt_in = lambda n, s, d: nc.dram_tensor(n, s, d, kind="ExternalInput").ap()
    xT = dt_in("xT", [C, T], F32)
    xres = dt_in("xres", [T, DH], F32)
    wq = dt_in("wq", [HPC, 128, CK, 128], F32)
    wk = dt_in("wk", [HPC, 128, CK, 128], F32)
    wv = dt_in("wv", [HPC, 128, CK, 128], F32)
    wo = dt_in("wo", [4, CK, 128, 512], F32)
    wf = dt_in("wf", [4, CK, 128, 512], F32)
    cent = dt_in("cent", [C, 64], F32)
    biasd = dt_in("biasd", [1, DH], F32)
    assignd = dt_in("assignd", [1, DH], I32)
    iota64 = dt_in("iota64", [64, 1], F32)
    idf = dt_in("idf", [128, 128], F32)
    onesf = dt_in("onesf", [128, 128], F32)
    onesb = dt_in("onesb", [128, 1], F32)
    cosT = dt_in("cosT", [128, T], F32)
    sinT2 = dt_in("sinT2", [128, T], F32)
    masks = dt_in("masks", [4, 128, 512], BF16)
    consts = dt_in("consts", [128, 8], F32)

    out = nc.dram_tensor("out", [T, DH], F32, kind="ExternalOutput").ap()

    dt_sc = lambda n, s, d: nc.dram_tensor(n, s, d, kind="Internal").ap()
    qts = dt_sc("qts", [HPC, 128, T], BF16)
    kts = dt_sc("kts", [HPC, 128, T], BF16)
    vfs = dt_sc("vfs", [HPC, 128, T], BF16)
    ffng = dt_sc("ffng", [T, DH], F32)
    ccin = dt_sc("ccin", [HPC * 128, T], BF16)
    ccout = dt_sc("ccout", [2 * HPC * 128, T], BF16)
    cmin = dt_sc("cmin", [64, 1], F32)
    cmout = dt_sc("cmout", [64, 1], F32)

    ccin_r = ccin.rearrange("(h p) t -> h p t", p=128)

    es = contextlib.ExitStack()
    sb = lambda n, s, d: es.enter_context(nc.sbuf_tensor(n, s, d))
    with es:
        XA = sb("XA", [128, CK * T], BF16)
        SCRF = sb("SCRF", [128, 6144], F32)
        SCRB = sb("SCRB", [128, 2048], BF16)
        WR = sb("WR", [128, 6144], BF16)
        cosb = sb("cosb", [128, T], F32)
        sinb = sb("sinb", [128, T], F32)
        maskb = sb("maskb", [128, 2048], BF16)
        DOTS = sb("DOTS", [64, 2048], F32)
        constsb = sb("constsb", [128, 8], F32)
        onesfb = sb("onesfb", [128, 128], F32)
        onesbb = sb("onesbb", [128, 1], BF16)
        iotab = sb("iotab", [64, 1], F32)
        idfb = sb("idfb", [128, 128], F32)
        qgcol = sb("qgcol", [128, 16], F32)
        cmcol = sb("cmcol", [64, 1], F32)
        cmgb = sb("cmgb", [64, 1], F32)
        ROWB = sb("ROWB", [128, 4096], BF16)

        psA = es.enter_context(nc.psum_tensor([128, 2048], F32))
        psB = es.enter_context(nc.psum_tensor([128, 2048], F32))

        sm = lambda n: Sem(es.enter_context(nc.semaphore(n)))
        cc = sm("cc")
        mP = sm("mP")
        mA = sm("mA")
        mV = sm("mV")

        TE, AC, VE, GP, SY = EQ("TE"), EQ("AC"), EQ("VE"), EQ("GP"), EQ("SY")
        WAITED = {}

        def W(q, s, v):
            if v is None or v <= 0:
                return
            k = (q.name, id(s))
            if WAITED.get(k, -1) >= v:
                return
            WAITED[k] = v
            q(lambda e, _h=s.h, _v=v: e.wait_ge(_h, _v))

        def OP(q, build, s=None, amt=1):
            if s is None:
                q(build)
                return None
            s.n += amt
            tgt = s.n
            q(lambda e, _b=build, _h=s.h, _a=amt: _b(e).then_inc(_h, _a))
            return tgt

        class LanePool:

            def __init__(self, q, nlanes, name):
                self.q = q
                self.sems = [sm(f"{name}{i}") for i in range(nlanes)]
                self.i = 0

            def dma(self, build):
                s = self.sems[self.i % len(self.sems)]
                self.i += 1
                if s.n > 0:
                    W(self.q, s, s.n)
                tgt = OP(self.q, build, s, 16)
                return (s, tgt)

            def mark(self):
                return [(s, s.n) for s in self.sems if s.n > 0]

        def WD(q, tok):
            if tok is None:
                return
            if isinstance(tok, list):
                for s, v in tok:
                    W(q, s, v)
            else:
                W(q, tok[0], tok[1])

        PG = LanePool(GP, 10, "dG")
        PS = LanePool(SY, 10, "dS")
        PA = LanePool(AC, 8, "dA")

        def xa(k):
            return XA[:, k * T:(k + 1) * T]

        smalls = [
            (cosb[:, :], cosT), (sinb[:, :], sinT2),
            (maskb[:, :].rearrange("p (m j) -> p m j", j=512),
             masks.rearrange("m p j -> p m j")),
            (constsb[:, :], consts), (onesfb[:, :], onesf),
            (iotab[:, :], iota64), (idfb[:, :], idf),
        ]
        tok_small = []
        for o_, i_ in smalls:
            tok_small.append(PS.dma(
                lambda e, _o=o_, _i=i_: e.dma_start(out=_o, in_=_i)))
        tok_onesb = PG.dma(lambda e: e.dma_start(out=onesbb[:, :], in_=onesb))
        tok_bias = PG.dma(lambda e: e.dma_start(out=ROWB[0:1, 0:2048], in_=biasd))
        tok_cent = PG.dma(lambda e: e.dma_start(
            out=WR[:, 0:2048].rearrange("p (k j) -> p k j", j=64),
            in_=cent.rearrange("(k p) j -> p k j", p=128)))
        tok_xa = []
        for k in range(CK):
            tok_xa.append(PG.dma(lambda e, _k=k: e.dma_start(
                out=xa(_k), in_=xT[_k * 128:(_k + 1) * 128, :])))

        WD(AC, tok_onesb)
        t_sq_mm = [None] * CK
        for k in range(CK):
            WD(AC, tok_xa[k])
            if k >= 1:
                W(AC, mP, t_sq_mm[k - 1])
            t_sq = OP(AC, lambda e, _k=k: e.activation(
                SCRB[:, :], xa(_k), AF.Square), mA, 1)
            WD(TE, tok_onesb)
            WD(TE, tok_xa[k])
            for j in range(4):
                OP(TE, lambda e, _k=k, _j=j: e.matmul(
                    psA[0:1, _j * 512:(_j + 1) * 512], onesbb[:, :],
                    xa(_k)[:, _j * 512:(_j + 1) * 512],
                    start=(_k == 0), stop=(_k == CK - 1)))
            W(TE, mA, t_sq)
            for j in range(4):
                t_sq_mm[k] = OP(TE, lambda e, _k=k, _j=j: e.matmul(
                    psB[0:1, _j * 512:(_j + 1) * 512], onesbb[:, :],
                    SCRB[:, _j * 512:(_j + 1) * 512],
                    start=(_k == 0), stop=(_k == CK - 1)),
                    mP if j == 3 else None, 1)
        t_stats = t_sq_mm[CK - 1]

        mean_r = SCRF[0:1, 0:2048]
        rstd_r = SCRF[0:1, 2048:4096]
        tmp_r = SCRF[0:1, 4096:6144]
        u_r = DOTS[0:1, 0:2048]
        W(VE, mP, t_stats)
        t_mean = OP(VE, lambda e: e.tensor_single_scalar(
            mean_r, psA[0:1, :], 1.0 / C, ALU.mult), mV, 1)
        t_msq = OP(VE, lambda e: e.tensor_single_scalar(
            tmp_r, psB[0:1, :], 1.0 / C, ALU.mult), mV, 1)
        W(VE, mV, t_msq)
        t_m2 = OP(VE, lambda e: e.tensor_tensor(
            rstd_r, mean_r, mean_r, ALU.mult), mV, 1)
        W(VE, mV, t_m2)
        t_var = OP(VE, lambda e: e.tensor_tensor(
            tmp_r, tmp_r, rstd_r, ALU.subtract), mV, 1)
        WD(AC, tok_small[3])
        W(AC, mV, t_var)
        t_ln = OP(AC, lambda e: e.activation(
            u_r, tmp_r, AF.Ln, bias=constsb[0:1, 4:5]), mA, 1)
        W(AC, mA, t_ln)
        t_rstd = OP(AC, lambda e: e.activation(
            rstd_r, u_r, AF.Exp, scale=constsb[0:1, 5:6]), mA, 1)

        WD(TE, tok_small[4])
        W(TE, mV, t_msq)
        t_mb = None
        for j in range(4):
            t_mb = OP(TE, lambda e, _j=j: e.matmul(
                psB[:, _j * 512:(_j + 1) * 512], onesfb[0:1, 0:128],
                mean_r[:, _j * 512:(_j + 1) * 512], start=True, stop=True),
                mP if j == 3 else None, 1)
        MBUF = WR[:, 2048:4096]
        W(AC, mP, t_mb)
        t_mbc = OP(AC, lambda e: e.activation(
            MBUF, psB[:, 0:2048], AF.Copy), mA, 1)
        W(TE, mA, t_rstd)
        W(TE, mA, t_mbc)
        t_bcast = None
        for j in range(4):
            t_bcast = OP(TE, lambda e, _j=j: e.matmul(
                psB[:, _j * 512:(_j + 1) * 512], onesfb[0:1, 0:128],
                rstd_r[:, _j * 512:(_j + 1) * 512], start=True, stop=True),
                mP if j == 3 else None, 1)

        tok_xf = [None] * CK
        t_xn = [None] * CK
        t_sub = [None] * CK
        t_dmm = None
        for k in range(CK):
            slot = SCRF[:, (k % 2) * 2048:(k % 2) * 2048 + 2048]
            if k >= 2:
                W(SY, mV, t_xn[k - 2])
            else:
                W(SY, mP, t_bcast)
            tok_xf[k] = PS.dma(lambda e, _k=k, _s=slot: e.dma_start(
                out=_s, in_=xT[_k * 128:(_k + 1) * 128, :]))
            WD(VE, tok_xf[k])
            W(VE, mA, t_mbc)
            W(VE, mP, t_bcast)
            t_sub[k] = OP(VE, lambda e, _s=slot: e.tensor_tensor(
                _s, _s, MBUF, ALU.subtract), mV, 1)
            W(VE, mV, t_sub[k])
            t_xn[k] = OP(VE, lambda e, _k=k, _s=slot: e.tensor_tensor(
                xa(_k), _s, psB[:, 0:2048], ALU.mult), mV, 1)
            W(TE, mV, t_xn[k])
            WD(TE, tok_cent)
            for j in range(4):
                t_dmm = OP(TE, lambda e, _k=k, _j=j: e.matmul(
                    psA[0:64, _j * 512:(_j + 1) * 512],
                    WR[:, _k * 64:(_k + 1) * 64],
                    xa(_k)[:, _j * 512:(_j + 1) * 512],
                    start=(_k == 0), stop=(_k == CK - 1)),
                    mP if (k == CK - 1 and j == 3) else None, 1)

        W(AC, mP, t_dmm)
        t_dexp = OP(AC, lambda e: e.activation(
            DOTS[:, :], psA[0:64, 0:2048], AF.Exp,
            bias=constsb[0:64, 3:4], scale=constsb[0:64, 2:3]), mA, 1)
        W(TE, mA, t_dexp)
        W(TE, mV, t_xn[CK - 1])
        t_dsum = None
        for j in range(4):
            t_dsum = OP(TE, lambda e, _j=j: e.matmul(
                psB[0:1, _j * 512:(_j + 1) * 512], onesfb[0:64, 0:1],
                DOTS[:, _j * 512:(_j + 1) * 512], start=True, stop=True),
                mP if j == 3 else None, 1)
        W(VE, mP, t_dsum)
        t_thr = OP(VE, lambda e: e.tensor_single_scalar(
            tmp_r, psB[0:1, :], THR, ALU.mult), mV, 1)
        W(TE, mV, t_thr)
        t_thb = None
        for j in range(4):
            t_thb = OP(TE, lambda e, _j=j: e.matmul(
                psB[0:64, _j * 512:(_j + 1) * 512], onesfb[0:1, 0:64],
                tmp_r[:, _j * 512:(_j + 1) * 512], start=True, stop=True),
                mP if j == 3 else None, 1)
        W(VE, mP, t_thb)
        t_diff = OP(VE, lambda e: e.tensor_tensor(
            DOTS[:, :], DOTS[:, :], psB[0:64, 0:2048], ALU.subtract), mV, 1)
        W(VE, mV, t_diff)
        t_cm = OP(VE, lambda e: e.reduce_max(
            cmcol[:, :], DOTS[:, :], axis=AX.X), mV, 1)
        W(SY, mV, t_cm)
        tok_cmw = PS.dma(lambda e: e.dma_start(out=cmin, in_=cmcol[:, :]))
        WD(GP, tok_cmw)
        t_cc1 = OP(GP, lambda e: e.collective_compute(
            "AllReduce", ALU.max, replica_groups=[list(range(8))],
            ins=[cmin.opt()], outs=[cmout.opt()]), cc, 1)

        WD(TE, tok_small[6])
        t_qtr = [None] * 16
        for tt in range(16):
            W(TE, mV, t_cm)
            t_qtr[tt] = OP(TE, lambda e, _tt=tt: e.transpose(
                psA[:, _tt * 64:(_tt + 1) * 64],
                DOTS[:, _tt * 128:(_tt + 1) * 128], idfb[0:64, 0:64]), mP, 1)
        t_qg = [None] * 16
        for tt in range(16):
            W(VE, mP, t_qtr[15])
            t1 = OP(VE, lambda e, _tt=tt: e.reduce_max(
                qgcol[:, _tt:_tt + 1], psA[:, _tt * 64:(_tt + 1) * 64],
                axis=AX.X), mV, 1)
            W(VE, mV, t1)
            t_qg[tt] = OP(VE, lambda e, _tt=tt: e.tensor_single_scalar(
                qgcol[:, _tt:_tt + 1], qgcol[:, _tt:_tt + 1], 0.0, ALU.is_gt),
                mV, 1)
        t_qr = None
        for tt in range(16):
            W(TE, mV, t_qg[tt])
            t_qr = OP(TE, lambda e, _tt=tt: e.matmul(
                psB[0:1, _tt * 128:(_tt + 1) * 128], qgcol[:, _tt:_tt + 1],
                idfb[:, :], start=True, stop=True),
                mP if tt == 15 else None, 1)
        W(AC, mP, t_qr)
        t_qrow = OP(AC, lambda e: e.activation(
            ROWB[0:1, 2048:4096], psB[0:1, 0:2048], AF.Copy), mA, 1)

        QF = SCRF[:, 0:2048]
        ROT = SCRF[:, 2048:4096]
        ROT2 = SCRF[:, 4096:6144]
        QRO = SCRB[:, 0:2048]

        mats = []
        for h in range(HPC):
            mats.append(("q", h, wq, qts))
            mats.append(("k", h, wk, kts))
            mats.append(("v", h, wv, vfs))

        wslot_owner = [None, None, None]
        t_mat = [None] * len(mats)
        t_rope = [None] * len(mats)
        t_ract = [None] * len(mats)
        tok_spill = [None] * len(mats)
        t_rot_read = 0
        t_qf_read = 0
        for mi, (kind, h, wsrc, dst) in enumerate(mats):
            tok_wh = [None, None]
            for half in range(2):
                sl = (2 * mi + half) % 3
                if wslot_owner[sl] is not None:
                    W(GP, mP, wslot_owner[sl])
                tok_wh[half] = PG.dma(lambda e, _w=wsrc, _h=h, _hf=half, _sl=sl:
                    e.dma_start(
                        out=WR[:, _sl * 2048:(_sl + 1) * 2048].rearrange(
                            "p (k j) -> p k j", j=128),
                        in_=_w[_h][:, _hf * 16:(_hf + 1) * 16, :]))

            psP = psA if (mi % 2 == 0) else psB
            if mi == 0:
                W(TE, mV, t_qg[15])
            elif mi == 1:
                W(TE, mA, t_qrow)
                W(TE, mV, t_diff)
            else:
                W(TE, mV, t_rope[mi - 2])
                W(TE, mA, t_ract[mi - 2])
            t_half_done = [None, None]
            for k in range(CK):
                half = k // 16
                sl = (2 * mi + half) % 3
                WD(TE, tok_wh[half])
                for j in range(4):
                    inc = (j == 3 and (k == 15 or k == CK - 1))
                    t = OP(TE, lambda e, _p=psP, _k=k, _j=j, _sl=sl:
                        e.matmul(
                            _p[:, _j * 512:(_j + 1) * 512],
                            WR[:, _sl * 2048 + (_k % 16) * 128:
                               _sl * 2048 + (_k % 16) * 128 + 128],
                            xa(_k)[:, _j * 512:(_j + 1) * 512],
                            start=(_k == 0), stop=(_k == CK - 1)),
                        mP if inc else None, 1)
                    if inc:
                        t_half_done[half] = t
            t_mat[mi] = t_half_done[1]
            wslot_owner[(2 * mi) % 3] = t_half_done[0]
            wslot_owner[(2 * mi + 1) % 3] = t_half_done[1]

            if kind == "v":
                W(AC, mP, t_mat[mi])
                if mi >= 1:
                    WD(AC, tok_spill[mi - 1])
                t_ract[mi] = OP(AC, lambda e, _p=psP: e.activation(
                    QRO, _p[:, 0:2048], AF.Copy), mA, 1)
                t_rope[mi] = mV.n
                W(SY, mA, t_ract[mi])
                tok_spill[mi] = PS.dma(lambda e, _h=h, _d=dst: e.dma_start(
                    out=_d[_h], in_=QRO))
            else:
                W(AC, mP, t_mat[mi])
                W(AC, mV, t_rot_read)
                OP(AC, lambda e, _p=psP: e.activation(
                    ROT[0:64, :], _p[64:128, 0:2048], AF.Copy))
                t_ract[mi] = OP(AC, lambda e, _p=psP: e.activation(
                    ROT[64:128, :], _p[0:64, 0:2048], AF.Copy), mA, 1)
                WD(VE, tok_small[0])
                WD(VE, tok_small[1])
                W(VE, mP, t_mat[mi])
                W(VE, mV, t_qf_read)
                W(VE, mA, t_ract[mi])
                t1 = OP(VE, lambda e, _p=psP: e.tensor_tensor(
                    QF, _p[:, 0:2048], cosb[:, :], ALU.mult), mV, 1)
                t_rope[mi] = t1
                W(VE, mA, t_ract[mi])
                t2 = OP(VE, lambda e: e.tensor_tensor(
                    ROT2, ROT, sinb[:, :], ALU.mult), mV, 1)
                t_rot_read = t2
                W(VE, mV, t2)
                if mi >= 1:
                    WD(VE, tok_spill[mi - 1])
                t3 = OP(VE, lambda e: e.tensor_tensor(
                    QRO, QF, ROT2, ALU.add), mV, 1)
                t_qf_read = t3
                W(SY, mV, t3)
                tok_spill[mi] = PS.dma(lambda e, _h=h, _d=dst: e.dma_start(
                    out=_d[_h], in_=QRO))

        W(SY, cc, t_cc1)
        tok_cmr = PS.dma(lambda e: e.dma_start(out=cmgb[:, :], in_=cmout))
        assignf = SCRF[0:1, 0:2048]
        WD(GP, tok_spill[-2])
        tok_asf = PG.dma(lambda e: e.dma_start(out=assignf, in_=assignd))
        WD(TE, tok_asf)
        W(TE, mA, t_ract[-1])
        W(TE, mV, t_rope[-1] or 0)
        t_ab = None
        for j in range(4):
            t_ab = OP(TE, lambda e, _j=j: e.matmul(
                psB[0:64, _j * 512:(_j + 1) * 512], onesfb[0:1, 0:64],
                assignf[:, _j * 512:(_j + 1) * 512], start=True, stop=True),
                mP if j == 3 else None, 1)
        WD(VE, tok_small[5])
        W(VE, mP, t_ab)
        onehot = SCRF[0:64, 0:2048]
        t_oh = OP(VE, lambda e: e.tensor_scalar(
            onehot, psB[0:64, 0:2048], iotab[:, :], None, ALU.is_equal), mV, 1)
        W(TE, mV, t_oh)
        WD(TE, tok_cmr)
        t_rgm = None
        for j in range(4):
            t_rgm = OP(TE, lambda e, _j=j: e.matmul(
                psB[0:1, _j * 512:(_j + 1) * 512], cmgb[:, :],
                onehot[:, _j * 512:(_j + 1) * 512], start=True, stop=True),
                mP if j == 3 else None, 1)
        W(VE, mP, t_rgm)
        t_rg01 = OP(VE, lambda e: e.tensor_single_scalar(
            tmp_r, psB[0:1, :], 0.0, ALU.is_gt), mV, 1)
        W(TE, mV, t_rg01)
        t_rgb = None
        for j in range(4):
            t_rgb = OP(TE, lambda e, _j=j: e.matmul(
                psB[:, _j * 512:(_j + 1) * 512], onesfb[0:1, 0:128],
                tmp_r[:, _j * 512:(_j + 1) * 512], start=True, stop=True),
                mP if j == 3 else None, 1)
        W(AC, mP, t_rgb)
        rgb = SCRF[:, 0:2048]
        t_rgbc = OP(AC, lambda e: e.activation(
            rgb, psB[:, 0:2048], AF.Copy), mA, 1)

        def ptile(tt):
            return (psA if tt < 4 else psB)[:, (tt % 4) * 512:(tt % 4) * 512 + 512]

        wtile_owner = [t_mat[-1]] * 12
        tok_stage = [None, None]
        t_ffn_evac = {}
        t_ffn_xa_low = None
        t_ffn_xa_low2 = None
        for th in range(2):
            for dch in range(4):
                gi = th * 4 + dch
                t_g = [None] * 8
                for k in range(CK):
                    sl = (gi * CK + k) % 12
                    if wtile_owner[sl] is not None:
                        W(GP, mP, wtile_owner[sl])
                    tok_wt = PG.dma(lambda e, _d=dch, _k=k, _sl=sl:
                        e.dma_start(out=WR[:, _sl * 512:(_sl + 1) * 512],
                                    in_=wf[_d, _k]))
                    WD(TE, tok_wt)
                    if gi == 0 and k == 0:
                        W(TE, mA, t_rgbc)
                    for tt in range(8):
                        if gi >= 1 and k == 0:
                            W(TE, mV, t_ffn_evac.get((gi - 1, tt)))
                        t_g[tt] = OP(TE, lambda e, _th=th, _k=k, _tt=tt, _sl=sl:
                            e.matmul(
                                ptile(_tt),
                                XA[:, _k * T + (_th * 8 + _tt) * 128:
                                   _k * T + (_th * 8 + _tt) * 128 + 128],
                                WR[:, _sl * 512:(_sl + 1) * 512],
                                start=(_k == 0), stop=False),
                            mP if tt == 7 else None, 1)
                    wtile_owner[sl] = t_g[7]
                    if gi == 7 and k == 23:
                        t_ffn_xa_low = t_g[7]
                    if gi == 7 and k == 26:
                        t_ffn_xa_low2 = t_g[7]
                W(TE, mA, t_qrow)
                WD(TE, tok_bias)
                for tt in range(8):
                    t_g[tt] = OP(TE, lambda e, _th=th, _d=dch, _tt=tt: e.matmul(
                        ptile(_tt),
                        ROWB[0:1, 2048 + (_th * 8 + _tt) * 128:
                             2048 + (_th * 8 + _tt) * 128 + 128],
                        ROWB[0:1, _d * 512:(_d + 1) * 512],
                        start=False, stop=True), mP, 1)
                for tt in range(8):
                    ss = tt % 2
                    stage = SCRF[:, 2048 + ss * 512:2048 + ss * 512 + 512]
                    W(VE, mP, t_g[tt])
                    W(VE, mA, t_rgbc)
                    WD(VE, tok_stage[ss])
                    t_ffn_evac[(gi, tt)] = OP(VE, lambda e, _th=th, _d=dch,
                                              _tt=tt, _st=stage:
                        e.scalar_tensor_tensor(
                            _st, ptile(_tt),
                            qgcol[:, _th * 8 + _tt:_th * 8 + _tt + 1],
                            rgb[:, _d * 512:(_d + 1) * 512],
                            ALU.mult, ALU.mult), mV, 1)
                    W(AC, mV, t_ffn_evac[(gi, tt)])
                    tok_stage[ss] = PA.dma(lambda e, _th=th, _d=dch, _tt=tt,
                                           _st=stage: e.dma_start(
                        out=ffng[(_th * 8 + _tt) * 128:(_th * 8 + _tt) * 128 + 128,
                                 _d * 512:(_d + 1) * 512], in_=_st))
        tok_ffn_done = PA.mark()
        t_ffn_pe = mP.n
        t_ffn_ve = t_ffn_evac[(7, 7)]

        AB = 43008
        def hslot(h, which):
            base = AB + (h % 2) * 6144
            off = {"q": 0, "k": 2048, "v": 4096}[which]
            return XA[:, base + off:base + off + 2048]

        def pslot(kc):
            return XA[:, AB + 12288 + kc * 512:AB + 12288 + kc * 512 + 512]

        def aoslot(par):
            return XA[:, AB + 20480 + par * 512:AB + 20480 + par * 512 + 512]

        tok_head_load = [None] * HPC
        t_head_pe = [None] * HPC
        t_exp = {}
        t_msk = {}
        t_avdone = {}
        t_csread = {}
        t_rbs_d = {}
        t_rc_d = {}
        tok_aow = {}
        last_exp_on_bank = [None] * 4
        last_av_on_slot = [None] * 16
        t_rbc_last = None
        recrow = lambda par: SCRF[0:1, 3072 + par * 512:3072 + par * 512 + 512]
        rbs = lambda par: SCRF[:, 4096 + par * 512:4096 + par * 512 + 512]

        def emit_head_load(h):
            WD(SY, tok_spill[3 * h])
            WD(SY, tok_spill[3 * h + 1])
            WD(SY, tok_spill[3 * h + 2])
            W(SY, mP, t_head_pe[h - 2] if h >= 2 else
              (t_ffn_xa_low if h == 0 else t_ffn_xa_low2))
            PS.dma(lambda e, _h=h: e.dma_start(out=hslot(_h, "q"), in_=qts[_h]))
            PS.dma(lambda e, _h=h: e.dma_start(out=hslot(_h, "k"), in_=kts[_h]))
            for kt in range(16):
                PS.dma(lambda e, _h=h, _kt=kt: e.dma_start(
                    out=hslot(_h, "v")[:, _kt * 128:(_kt + 1) * 128],
                    in_=vfs[_h][:, _kt * 128:(_kt + 1) * 128],
                    transpose=True))
            tok_head_load[h] = PS.mark()

        W(TE, mV, t_ffn_ve)
        pend_rbc = [None]
        tok_aT_chunk = {}
        AG_START = [0, 6, 12, 15]
        AG_NH = [6, 6, 3, 1]
        AG_OUT = [0, 1536, 3072, 3840]
        AT_EARLY = [(0, 0, 0, 6), (1, 0, 6, 6), (2, 0, 12, 3), (3, 0, 15, 1),
                    (0, 1, 16, 5)]
        AT_LATE = [(0, 1, 21, 1), (1, 1, 22, 6), (2, 1, 28, 3), (3, 1, 31, 1)]

        def emit_ag(i):
            if pend_rbc[0] is not None:
                pend_rbc[0]()
                pend_rbc[0] = None
            for s_, v_ in PS.mark():
                W(GP, s_, v_)
            W(GP, cc, 1 + i)
            if i == 0:
                W(GP, mP, t_ffn_pe)
            OP(GP, lambda e, _i=i: e.collective_compute(
                "AllGather", ALU.bypass,
                replica_groups=[[0, 1], [2, 3], [4, 5], [6, 7]],
                ins=[ccin[AG_START[_i] * 128:(AG_START[_i] + AG_NH[_i]) * 128,
                          :].opt()],
                outs=[ccout[AG_OUT[_i]:AG_OUT[_i] + 2 * AG_NH[_i] * 128,
                            :].opt()]), cc, 1)
            for (ii, r, g0, nh) in AT_EARLY:
                if ii != i:
                    continue
                W(GP, cc, 2 + i)
                ro = (AG_OUT[ii] + r * AG_NH[ii] * 128
                      + (g0 - r * 16 - AG_START[ii]) * 128)
                tok = PG.dma(lambda e, _ro=ro, _g0=g0, _nh=nh:
                    e.dma_start(
                        out=XA[:, _g0 * 2048:(_g0 + _nh) * 2048].rearrange(
                            "p (h t) -> p h t", t=T),
                        in_=ccout[_ro:_ro + _nh * 128, :].rearrange(
                            "(h p) t -> p h t", p=128)))
                for g in range(g0, g0 + nh):
                    tok_aT_chunk[g] = tok

        W(TE, mV, t_ffn_ve)
        emit_head_load(0)
        for h in range(HPC):
            if h + 1 < HPC:
                emit_head_load(h + 1)
            qb, kb, vb = hslot(h, "q"), hslot(h, "k"), hslot(h, "v")
            t_sc = {}

            def AVP(qg):
                par = qg % 2
                return psB[:, (0 if par == 0 else 3) * 512:
                           (0 if par == 0 else 3) * 512 + 512]

            CSP = psB[0:1, 512:1024]

            def emit_sc(qg, kc, _h=h, _qb=qb, _kb=kb):
                WD(TE, tok_head_load[_h])
                W(TE, mA, last_exp_on_bank[kc % 4])
                t_sc[(qg, kc)] = OP(TE, lambda e, _kc=kc, _qg=qg: e.matmul(
                    psA[:, (_kc % 4) * 512:(_kc % 4) * 512 + 512],
                    _kb[:, _kc * 128:(_kc + 1) * 128],
                    _qb[:, _qg * 512:(_qg + 1) * 512],
                    start=True, stop=True), mP, 1)

            def emit_exp(qg, kc, _h=h):
                W(AC, mP, t_sc[(qg, kc)])
                W(AC, mP, last_av_on_slot[kc])
                t_exp[(_h, qg, kc)] = OP(AC, lambda e, _kc=kc: e.activation(
                    pslot(_kc), psA[:, (_kc % 4) * 512:(_kc % 4) * 512 + 512],
                    AF.Exp, bias=constsb[:, 1:2], scale=constsb[:, 0:1]),
                    mA, 1)
                last_exp_on_bank[kc % 4] = t_exp[(_h, qg, kc)]
                if kc >= 4 * qg:
                    m = kc - 4 * qg
                    W(VE, mA, t_exp[(_h, qg, kc)])
                    t_msk[(_h, qg, kc)] = OP(VE, lambda e, _kc=kc, _m=m:
                        e.tensor_tensor(
                            pslot(_kc), pslot(_kc),
                            maskb[:, _m * 512:(_m + 1) * 512], ALU.mult),
                        mV, 1)

            def emit_cs(qg, kc, start, _h=h):
                kcn = 4 * (qg + 1)
                if start:
                    pr = (_h, qg - 1) if qg >= 1 else (
                        (_h - 1, 3) if _h >= 1 else None)
                    if pr is not None and pr in t_rc_d:
                        W(TE, mV, t_rc_d[pr])
                return OP(TE, lambda e, _kc=kc, _st=start: e.matmul(
                    CSP, onesbb[:, :], pslot(_kc),
                    start=_st, stop=(_kc == kcn - 1)), mP, 1)

            def emit_csav(qg, kc, _h=h, _vb=vb):
                kcn = 4 * (qg + 1)
                avp = AVP(qg)
                if (_h, qg, kc) in t_msk:
                    W(TE, mV, t_msk[(_h, qg, kc)])
                else:
                    W(TE, mA, t_exp[(_h, qg, kc)])
                if kc == 0:
                    pk = (_h, qg - 2) if qg >= 2 else (
                        (_h - 1, qg + 2) if _h >= 1 else None)
                    if pk is not None:
                        W(TE, mV, t_avdone.get(pk))
                if kc == 1:
                    emit_cs(qg, 1, True)
                    last_av_on_slot[0] = emit_cs(qg, 0, False)
                elif kc >= 2 or kcn == 1:
                    emit_cs(qg, kc, False)
                t = OP(TE, lambda e, _kc=kc, _kcn=kcn, _avp=avp: e.matmul(
                    _avp, _vb[:, _kc * 128:(_kc + 1) * 128], pslot(_kc),
                    start=(_kc == 0), stop=(_kc == _kcn - 1)), mP, 1)
                last_av_on_slot[kc] = t
                return t

            for j in range(4):
                emit_sc(0, j)
            for qg in range(NQG):
                kcn = 4 * (qg + 1)
                par = qg % 2
                avp = AVP(qg)
                t_last_av = None
                for kc in range(kcn):
                    emit_exp(qg, kc)
                    t_last_av = emit_csav(qg, kc)
                    if kc == 1 and pend_rbc[0] is not None:
                        t_rbc_last = pend_rbc[0]()
                        pend_rbc[0] = None
                    nxt = kc + 4
                    if nxt < kcn:
                        emit_sc(qg, nxt)
                    elif qg + 1 < NQG and nxt - kcn < 4:
                        emit_sc(qg + 1, nxt - kcn)
                t_csread[(h, qg)] = t_last_av
                t_head_pe[h] = t_last_av

                W(VE, mP, t_last_av)
                t_rc = OP(VE, lambda e, _p=par: e.reciprocal(
                    recrow(_p), CSP), mV, 1)
                t_rc_d[(h, qg)] = t_rc

                def emit_rbc(_h=h, _qg=qg, _par=par, _avp=avp, _trc=t_rc):
                    W(TE, mV, _trc)
                    prevq = (_h, _qg - 1) if _qg >= 1 else (
                        (_h - 1, 3) if _h >= 1 else None)
                    if prevq is not None and prevq in t_rbs_d:
                        W(TE, mV, t_rbs_d[prevq][0])
                    t_rbc = OP(TE, lambda e, _p=_par: e.matmul(
                        psB[:, 1024:1536], onesfb[0:1, 0:128], recrow(_p),
                        start=True, stop=True), mP, 1)
                    pk2 = (_h, _qg - 2) if _qg >= 2 else (
                        (_h - 1, _qg + 2) if _h >= 1 else None)
                    W(VE, mP, t_rbc)
                    if pk2 is not None:
                        W(VE, mV, t_avdone.get(pk2))
                    t_rbs = OP(VE, lambda e, _p=_par: e.tensor_copy(
                        rbs(_p), psB[:, 1024:1536]), mV, 1)
                    t_rbs_d[(_h, _qg)] = (t_rbs, t_rbc, t_rc)
                    W(VE, mV, t_rbs)
                    if pk2 is not None:
                        WD(VE, tok_aow.get(pk2))
                    t_av = OP(VE, lambda e, _p=_par, _a=_avp: e.tensor_tensor(
                        aoslot(_p), _a, rbs(_p), ALU.mult), mV, 1)
                    t_avdone[(_h, _qg)] = t_av
                    W(SY, mV, t_av)
                    tok_aow[(_h, _qg)] = PS.dma(lambda e, _hh=_h, _qq=_qg,
                                                _p=_par: e.dma_start(
                        out=ccin_r[_hh][:, _qq * 512:(_qq + 1) * 512],
                        in_=aoslot(_p)))
                    return t_rbc

                pend_rbc[0] = emit_rbc
                if qg == NQG - 1 and h in (5, 11, 14, 15):
                    emit_ag((5, 11, 14, 15).index(h))

        if pend_rbc[0] is not None:
            t_rbc_last = pend_rbc[0]()
            pend_rbc[0] = None

        W(SY, mP, t_head_pe[HPC - 1])
        W(SY, mV, t_avdone[(HPC - 1, NQG - 1)])
        WD(SY, tok_aow.get((HPC - 1, NQG - 1)))
        WD(SY, tok_aow.get((HPC - 1, NQG - 2)))
        for (i, r, g0, nh) in AT_LATE:
            W(SY, cc, 2 + i)
            ro = (AG_OUT[i] + r * AG_NH[i] * 128
                  + (g0 - r * 16 - AG_START[i]) * 128)
            tok = PS.dma(lambda e, _ro=ro, _g0=g0, _nh=nh:
                e.dma_start(
                    out=XA[:, _g0 * 2048:(_g0 + _nh) * 2048].rearrange(
                        "p (h t) -> p h t", t=T),
                    in_=ccout[_ro:_ro + _nh * 128, :].rearrange(
                        "(h p) t -> p h t", p=128)))
            for g in range(g0, g0 + nh):
                tok_aT_chunk[g] = tok

        KORD = [0, 1, 2, 3, 4, 5, 16, 17, 18, 19, 20, 21,
                6, 7, 8, 9, 10, 11, 22, 23, 24, 25, 26, 27,
                12, 13, 14, 28, 29, 30, 15, 31]
        XRt = lambda s: SCRF[:, s * 512:s * 512 + 512]
        FRt = lambda s: SCRF[:, 2048 + s * 512:2048 + s * 512 + 512]
        OUTt = lambda s: SCRF[:, 4096 + s * 512:4096 + s * 512 + 512]
        xr_owner = [None] * 4
        fr_owner = [None] * 4
        tok_out = [None] * 4
        t_o_evac = {}
        wtile_owner = [None] * 12
        tok_xr_g = {}
        tok_fr_g = {}

        def emit_s4_prefetch(j):
            gj, ttj = j // 8, j % 8
            thj, dchj = gj // 4, gj % 4
            sl = ttj % 4
            row = (thj * 8 + ttj) * 128
            col = dchj * 512
            if xr_owner[sl] is not None:
                W(AC, mV, xr_owner[sl])
            tok_xr_g[j] = PA.dma(lambda e, _r=row, _c=col, _sl=sl: e.dma_start(
                out=XRt(_sl), in_=xres[_r:_r + 128, _c:_c + 512]))
            if fr_owner[sl] is not None:
                W(AC, mV, fr_owner[sl])
            WD(AC, tok_ffn_done)
            tok_fr_g[j] = PA.dma(lambda e, _r=row, _c=col, _sl=sl: e.dma_start(
                out=FRt(_sl), in_=ffng[_r:_r + 128, _c:_c + 512]))

        W(AC, mV, t_avdone[(HPC - 1, NQG - 1)])
        W(AC, mP, t_rbc_last)
        for j in range(4):
            emit_s4_prefetch(j)
        for th in range(2):
            for dch in range(4):
                gi = th * 4 + dch
                t_g = [None] * 8
                for k in range(CK):
                    kc_g = KORD[k]
                    sl = (gi * CK + k) % 12
                    if wtile_owner[sl] is not None:
                        W(GP, mP, wtile_owner[sl])
                    tok_wt = PG.dma(lambda e, _d=dch, _k=kc_g, _sl=sl:
                        e.dma_start(out=WR[:, _sl * 512:(_sl + 1) * 512],
                                    in_=wo[_d, _k]))
                    WD(TE, tok_wt)
                    WD(TE, tok_aT_chunk[kc_g])
                    if gi == 0 and k == 0:
                        W(TE, mV, t_avdone[(HPC - 1, NQG - 1)])
                        W(TE, mV, t_rbs_d[(HPC - 1, NQG - 1)][0])
                        for bb in range(4):
                            W(TE, mA, last_exp_on_bank[bb])
                    for tt in range(8):
                        if gi >= 1 and k == 0:
                            W(TE, mV, t_o_evac.get((gi - 1, tt)))
                        t_g[tt] = OP(TE, lambda e, _th=th, _k=kc_g, _tt=tt, _sl=sl:
                            e.matmul(
                                ptile(_tt),
                                XA[:, _k * T + (_th * 8 + _tt) * 128:
                                   _k * T + (_th * 8 + _tt) * 128 + 128],
                                WR[:, _sl * 512:(_sl + 1) * 512],
                                start=(_k == 0), stop=(_k == CK - 1)),
                            mP if (tt == 7 or k == CK - 1) else None, 1)
                    wtile_owner[sl] = t_g[7]
                for tt in range(8):
                    sl = tt % 4
                    j = gi * 8 + tt
                    W(VE, mP, t_g[tt])
                    WD(VE, tok_xr_g[j])
                    WD(VE, tok_fr_g[j])
                    if tok_out[sl] is not None:
                        WD(VE, tok_out[sl])
                    t1 = OP(VE, lambda e, _tt=tt, _sl=sl: e.tensor_tensor(
                        OUTt(_sl), ptile(_tt), XRt(_sl), ALU.add), mV, 1)
                    W(VE, mV, t1)
                    t_o_evac[(gi, tt)] = OP(VE, lambda e, _sl=sl: e.tensor_tensor(
                        OUTt(_sl), OUTt(_sl), FRt(_sl), ALU.add), mV, 1)
                    xr_owner[sl] = t_o_evac[(gi, tt)]
                    fr_owner[sl] = t_o_evac[(gi, tt)]
                    W(AC, mV, t_o_evac[(gi, tt)])
                    tok_out[sl] = PA.dma(lambda e, _th=th, _d=dch, _tt=tt,
                                         _sl=sl: e.dma_start(
                        out=out[(_th * 8 + _tt) * 128:(_th * 8 + _tt) * 128 + 128,
                                _d * 512:(_d + 1) * 512], in_=OUTt(_sl)))
                    if j + 4 < 64:
                        emit_s4_prefetch(j + 4)

        endmark = PA.mark()
        WD(AC, endmark)
        WD(SY, endmark)

        with nc.Block() as block:
            @block.tensor
            def _(e):
                for f in TE.fns:
                    f(e)

            @block.scalar
            def _(e):
                for f in AC.fns:
                    f(e)

            @block.vector
            def _(e):
                for f in VE.fns:
                    f(e)

            @block.gpsimd
            def _(e):
                for f in GP.fns:
                    f(e)

            @block.sync
            def _(e):
                for f in SY.fns:
                    f(e)

    return nc


def _tile_w_heads(w_slice_T):
    return np.ascontiguousarray(
        w_slice_T.reshape(CK, 128, HPC, 128).transpose(2, 1, 0, 3))


def _tile_w_512(w_T_half):
    return np.ascontiguousarray(
        w_T_half.reshape(CK, 128, 4, 512).transpose(2, 0, 1, 3))


def _host_consts():
    pos = np.arange(T, dtype=np.float64)
    thetas = 1.0 / 10000.0 ** (np.arange(0, HD, 2, dtype=np.float64) / HD)
    pt = np.outer(pos, thetas)
    tab = np.concatenate([pt, pt], axis=1)
    cos = np.cos(tab).astype(np.float32).T
    sin = np.sin(tab).astype(np.float32).T
    sin2 = sin.copy()
    sin2[0:64, :] *= -1.0
    m, p, j = np.meshgrid(np.arange(4), np.arange(128), np.arange(512),
                          indexing="ij")
    masks = ((p + 128 * m) <= j).astype(ml_dtypes.bfloat16)
    consts = np.zeros((128, 8), dtype=np.float32)
    consts[:, 0] = SCALE
    consts[:, 1] = SSHIFT
    consts[:, 2] = DTEMP
    consts[:, 3] = DSHIFT
    consts[:, 4] = EPS
    consts[:, 5] = -0.5
    consts[:, 6] = 1.0 / C
    return (np.ascontiguousarray(cos), np.ascontiguousarray(sin2),
            np.ascontiguousarray(masks), consts)


_NC_CACHE = [None]


def kernel(L0_attn_x, L0_attn_wq, L0_attn_wk, L0_attn_wv, L0_attn_wo,
           L0_ffn_weight, L0_ffn_bias, L0_ffn_centroids, L0_ffn_assignments):
    x = np.asarray(L0_attn_x, dtype=np.float32)
    B = x.shape[0]
    cos, sin2, masks, consts = _host_consts()
    iota = np.arange(64, dtype=np.float32).reshape(64, 1)
    idf = np.eye(128, dtype=np.float32)
    onesf = np.ones((128, 128), dtype=np.float32)
    onesb = np.ones((128, 1), dtype=np.float32)
    cent = np.ascontiguousarray(np.asarray(L0_ffn_centroids, np.float32).T)

    wq_t, wk_t, wv_t, wo_t, wf_t, bias_h, asg_h = {}, {}, {}, {}, {}, {}, {}
    for hg in range(2):
        sl = slice(hg * DH, (hg + 1) * DH)
        wq_t[hg] = _tile_w_heads(np.asarray(L0_attn_wq, np.float32)[sl, :].T)
        wk_t[hg] = _tile_w_heads(np.asarray(L0_attn_wk, np.float32)[sl, :].T)
        wv_t[hg] = _tile_w_heads(np.asarray(L0_attn_wv, np.float32)[sl, :].T)
        wo_t[hg] = _tile_w_512(np.ascontiguousarray(
            np.asarray(L0_attn_wo, np.float32).T[:, sl]))
        wf_t[hg] = _tile_w_512(np.ascontiguousarray(
            np.asarray(L0_ffn_weight, np.float32).T[:, sl]))
        bias_h[hg] = np.ascontiguousarray(
            np.asarray(L0_ffn_bias, np.float32)[sl].reshape(1, DH))
        asg_h[hg] = np.ascontiguousarray(
            np.asarray(L0_ffn_assignments, np.int32)[sl].reshape(1, DH))

    xT_b = [np.ascontiguousarray(x[b].T) for b in range(B)]
    in_maps = []
    for c in range(8):
        b, hg = c // 2, c % 2
        in_maps.append({
            "xT": xT_b[b],
            "xres": np.ascontiguousarray(x[b][:, hg * DH:(hg + 1) * DH]),
            "wq": wq_t[hg], "wk": wk_t[hg], "wv": wv_t[hg],
            "wo": wo_t[hg], "wf": wf_t[hg],
            "cent": cent, "biasd": bias_h[hg], "assignd": asg_h[hg],
            "iota64": iota, "idf": idf, "onesf": onesf, "onesb": onesb,
            "cosT": cos, "sinT2": sin2, "masks": masks, "consts": consts,
        })

    if _NC_CACHE[0] is None:
        _NC_CACHE[0] = build_nc()
    nc = _NC_CACHE[0]
    res = run_bass_kernel_spmd(nc, in_maps, core_ids=list(range(8)))

    outp = np.empty((B, T, C), dtype=np.float32)
    for c in range(8):
        b, hg = c // 2, c % 2
        outp[b][:, hg * DH:(hg + 1) * DH] = res.results[c]["out"]
    return outp
